# version 20
# baseline (speedup 1.0000x reference)
import numpy as np
import ml_dtypes
from contextlib import ExitStack
import concourse.bass as bass
import concourse.mybir as mybir
from concourse.bass_utils import run_bass_kernel_spmd

F32 = mybir.dt.float32
BF16 = mybir.dt.bfloat16
AF = mybir.ActivationFunctionType
ALU = mybir.AluOpType
bfdt = ml_dtypes.bfloat16
AX = mybir.AxisListType.X

T = 2048
D = 1024
NH = 16
TT = T // 128
KD = D // 128
NC = 32
NPG = 32
EXP_SHIFT = -8.0

ENGINES = ("sync", "tensor", "vector", "scalar", "gpsimd")
HD = 4096


def _stage_vectors(weights_k, s):
    nb = HD // (2 * s)
    w = weights_k.reshape(nb, s, 2, 2)
    d0 = np.zeros(HD, dtype=np.float32)
    od = np.zeros(HD, dtype=np.float32)
    i = (np.arange(nb)[:, None] * 2 * s + np.arange(s)[None, :]).ravel()
    j = i + s
    d0[i] = w[:, :, 0, 0].ravel()
    d0[j] = w[:, :, 1, 1].ravel()
    od[i] = w[:, :, 0, 1].ravel()
    od[j] = w[:, :, 1, 0].ravel()
    return d0, od


def _stage_mats(weights):
    S_low = np.zeros((7, NC, 128, 128), dtype=np.float32)
    T_blk = np.zeros((5, NPG, 128, 128), dtype=np.float32)
    ar = np.arange(128)
    for k in range(7):
        s = 1 << k
        d0, od = _stage_vectors(weights[k], s)
        for c in range(NC):
            M = S_low[k, c]
            M[ar, ar] = d0[c * 128 + ar]
            M[ar, ar ^ s] = od[c * 128 + ar]
    p_lo = ar >> 5
    c_arr = ar & 31
    for k2 in range(5):
        s = 1 << (7 + k2)
        sh = s >> 7
        d0, od = _stage_vectors(weights[7 + k2], s)
        for pg in range(NPG):
            p = 4 * pg + p_lo
            f = c_arr * 128 + p
            M = T_blk[k2, pg]
            M[ar, ar] = d0[f]
            M[ar, ar ^ sh] = od[f]
    return S_low, T_blk


def _host_arrays(W_qkv, W_exp, b_exp, up_w, up_b, down_w, W_con, b_con, down_b):
    h = {}
    Wqk = W_qkv[:, 0:2 * D]
    h["wqk"] = np.ascontiguousarray(
        Wqk.reshape(KD, 128, 16, 128).transpose(2, 0, 1, 3))
    h["wv"] = np.ascontiguousarray(W_qkv[:, 2 * D:3 * D].reshape(KD, 128, D))
    h["wexpT"] = np.ascontiguousarray(
        W_exp.T.reshape(NC, 128, D).transpose(1, 0, 2))
    h["wconpt"] = np.ascontiguousarray(
        W_con.reshape(NC, NPG, 4, D).transpose(1, 2, 0, 3).reshape(NPG, 128, D)
        .transpose(1, 0, 2))
    S_u, T_u = _stage_mats(up_w)
    S_d, T_d = _stage_mats(down_w)
    def _mix(S, tset):
        out = S.copy()
        for k in tset:
            out[k] = S[k].transpose(0, 2, 1)
        return np.ascontiguousarray(out.transpose(2, 1, 0, 3))
    h["su"] = _mix(S_u, (1, 3, 5, 6))
    h["tu"] = _mix(T_u, (1, 3, 4))
    h["sdt"] = _mix(S_d, (1, 2, 4, 6))
    h["tdt"] = _mix(T_d, (1, 3, 4))
    h["bexp2d"] = np.ascontiguousarray(b_exp.reshape(NC, 128).T)
    h["ubiaspt"] = np.ascontiguousarray(
        up_b.reshape(NC, NPG, 4).transpose(2, 0, 1).reshape(128, NPG))
    h["bconrow"] = np.ascontiguousarray(b_con.reshape(1, D))
    h["dbiasrow"] = np.ascontiguousarray(down_b.reshape(1, D))
    h["identbf"] = np.eye(128, dtype=bfdt)
    h["identf"] = np.eye(128, dtype=np.float32)
    ar = np.arange(128)
    h["maskbf"] = np.where(ar[:, None] > ar[None, :], -1e5, 0.0).astype(bfdt)
    h["mask01bf"] = (ar[:, None] <= ar[None, :]).astype(bfdt)
    mw = np.zeros((4, 128, 512), dtype=np.float32)
    for o in range(4):
        mw[o, :, :o * 128] = -1e5
        mw[o, :, o * 128:(o + 1) * 128] = np.where(
            ar[:, None] > ar[None, :], -1e5, 0.0)
    h["maskwbf"] = mw.astype(bfdt)
    return h


class Prog:

    def __init__(self, nc, stack):
        self.nc = nc
        self.stack = stack
        self.sems = {}
        self.cnt = {}
        self.waited = {}
        self.streams = None

    def sem(self, name):
        if name not in self.sems:
            self.sems[name] = self.stack.enter_context(self.nc.semaphore(name))
            self.cnt[name] = 0
        return name

    def begin(self):
        self.streams = {e: [] for e in ENGINES}

    def wait(self, engine, sname, val):
        if val <= 0:
            return
        key = (engine, sname)
        if self.waited.get(key, 0) >= val:
            return
        self.waited[key] = val
        h = self.sems[sname]
        self.streams[engine].append(lambda eng, hh=h, v=val: eng.wait_ge(hh, v))

    def gwait(self, engine, gate):
        if gate is not None:
            self.wait(engine, gate[0], gate[1])

    def op(self, engine, builder, inc=None, n=1):
        if inc is not None:
            self.cnt[inc] += n
            h = self.sems[inc]
            self.streams[engine].append(
                lambda eng, b=builder, hh=h, m=n: b(eng).then_inc(hh, m))
            return self.cnt[inc]
        self.streams[engine].append(lambda eng, b=builder: b(eng))
        return None

    def drain(self, engine):
        self.streams[engine].append(lambda eng: eng.drain())

    def frame_barrier(self, sems):
        for e in ENGINES:
            for s in sems:
                if s in self.cnt:
                    self.wait(e, s, self.cnt[s])
            for s in ("dve", "act", "pe"):
                self.wait(e, s, self.cnt[s])
        for e in ("vector", "scalar"):
            self.drain(e)

    def flush(self, name):
        streams = self.streams
        self.streams = None
        with self.nc.Block(name) as block:
            for ename in ENGINES:
                ops = streams[ename]
                if not ops:
                    continue

                def body(eng, ops=ops):
                    for f in ops:
                        f(eng)
                getattr(block, ename)(body)


class Stream:

    def __init__(self, P, name, queue, nslots=2):
        self.P = P
        self.name = name
        self.queue = queue
        self.sems = [P.sem(f"{name}{i}") for i in range(nslots)]
        self.fills = [0] * nslots
        self.gates = [None] * nslots

    def fill(self, slot, builder):
        self.P.gwait(self.queue, self.gates[slot])
        self.P.op(self.queue, builder, self.sems[slot], 16)
        self.fills[slot] += 1

    def wait(self, engine, slot):
        self.P.wait(engine, self.sems[slot], 16 * self.fills[slot])

    def set_gate(self, slot, sname, val):
        self.gates[slot] = (sname, val)


def hap(t, offset, dims):
    th = t.tensor if isinstance(t, bass.AP) else t
    return bass.AP(th, offset, dims)


def _copy(eng, dst, src):
    if hasattr(eng, "tensor_copy"):
        return eng.tensor_copy(dst, src)
    return eng.copy(dst, src)


def build_nc(debug=False, prep_only=False, secmask=0xFFFF):
    nc = bass.Bass("TRN2", target_bir_lowering=False)
    st = ExitStack()
    P = Prog(nc, st)

    I = {}
    I["x"] = nc.dram_tensor("x", [T, D], F32, kind="ExternalInput").ap()
    I["wqk"] = nc.dram_tensor("wqk", [16, KD, 128, 128], F32, kind="ExternalInput").ap()
    I["wv"] = nc.dram_tensor("wv", [KD, 128, D], F32, kind="ExternalInput").ap()
    I["wexpT"] = nc.dram_tensor("wexpT", [128, NC, D], F32, kind="ExternalInput").ap()
    I["wconpt"] = nc.dram_tensor("wconpt", [128, NPG, D], F32, kind="ExternalInput").ap()
    I["su"] = nc.dram_tensor("su", [128, NC, 7, 128], F32, kind="ExternalInput").ap()
    I["tu"] = nc.dram_tensor("tu", [128, NPG, 5, 128], F32, kind="ExternalInput").ap()
    I["sdt"] = nc.dram_tensor("sdt", [128, NC, 7, 128], F32, kind="ExternalInput").ap()
    I["tdt"] = nc.dram_tensor("tdt", [128, NPG, 5, 128], F32, kind="ExternalInput").ap()
    I["bexp2d"] = nc.dram_tensor("bexp2d", [128, NC], F32, kind="ExternalInput").ap()
    I["ubiaspt"] = nc.dram_tensor("ubiaspt", [128, NPG], F32, kind="ExternalInput").ap()
    I["bconrow"] = nc.dram_tensor("bconrow", [1, D], F32, kind="ExternalInput").ap()
    I["dbiasrow"] = nc.dram_tensor("dbiasrow", [1, D], F32, kind="ExternalInput").ap()
    I["identbf"] = nc.dram_tensor("identbf", [128, 128], BF16, kind="ExternalInput").ap()
    I["identf"] = nc.dram_tensor("identf", [128, 128], F32, kind="ExternalInput").ap()
    I["maskbf"] = nc.dram_tensor("maskbf", [128, 128], BF16, kind="ExternalInput").ap()
    I["mask01bf"] = nc.dram_tensor("mask01bf", [128, 128], BF16, kind="ExternalInput").ap()
    I["maskwbf"] = nc.dram_tensor("maskwbf", [4, 128, 512], BF16, kind="ExternalInput").ap()

    w2t_s = nc.dram_tensor("w2t_s", [NPG, 128, KD * 128], BF16, kind="Internal").ap()
    w3a_s = nc.dram_tensor("w3a_s", [NPG, 128, D], BF16, kind="Internal").ap()
    r1_s = nc.dram_tensor("r1_s", [T, D], F32, kind="Internal").ap()
    y = nc.dram_tensor("y", [T, D], F32, kind="ExternalOutput").ap()

    DBG = {}
    if debug:
        for nm, shape, dt in [("dbg_xn1t", [128, KD * T], BF16),
                              ("dbg_qt", [128, 8 * T], BF16),
                              ("dbg_kt", [128, 8 * T], BF16),
                              ("dbg_v", [128, TT * NH * 65], BF16),
                              ("dbg_r1", [T, D], F32),
                              ("dbg_w2t", [NPG, 128, KD * 128], BF16),
                              ("dbg_w3a", [NPG, 128, D], BF16),
                              ("dbg_bias2", [128, NPG], F32),
                              ("dbg_h4", [128, NC * 1024], BF16)]:
            DBG[nm] = nc.dram_tensor(nm, shape, dt, kind="ExternalOutput").ap()

    for s in ("pe", "act", "dve", "ld1", "g1", "grl", "sdbg"):
        P.sem(s)

    ident_bf = st.enter_context(nc.sbuf_tensor("ident_bf", [128, 128], BF16))
    ident_f = st.enter_context(nc.sbuf_tensor("ident_f", [128, 128], F32))
    mask_bf = st.enter_context(nc.sbuf_tensor("mask_bf", [128, 128], BF16))
    mask01_bf = st.enter_context(nc.sbuf_tensor("mask01_bf", [128, 128], BF16))
    maskw_sb = st.enter_context(nc.sbuf_tensor("maskw_sb", [128, 4 * 512], BF16))
    consts = st.enter_context(nc.sbuf_tensor("consts", [128, 8], F32))
    CB_EXPB, CB_EXPS, CB_RD, CB_EPS, CB_ONE = 0, 1, 2, 3, 4
    bias3rep = st.enter_context(nc.sbuf_tensor("bias3rep", [128, D], F32))
    bias2_sb = st.enter_context(nc.sbuf_tensor("bias2_sb", [128, NPG], F32))
    bexp2d_sb = st.enter_context(nc.sbuf_tensor("bexp2d_sb", [128, NC], BF16))
    ubias_sb = st.enter_context(nc.sbuf_tensor("ubias_sb", [128, NPG], F32))

    def ccol(i):
        return consts[:, i:i + 1]

    def ln_tile(src_tile, dst_sb, dst_rowlen, dst_col, xsq, stats, xnbf, tp_ps):
        P.op("scalar", lambda s: s.square(xsq[:], src_tile), "act", 1)
        P.op("vector", lambda v: v.reduce_sum(stats[:, 0:1], src_tile, AX))
        P.wait("vector", "act", P.cnt["act"])
        P.op("vector", lambda v: v.reduce_sum(stats[:, 1:2], xsq[:], AX))
        P.drain("vector")
        P.op("vector", lambda v: v.tensor_scalar(
            stats[:, 2:3], stats[:, 0:1], ccol(CB_RD), None, ALU.mult))
        P.op("vector", lambda v: v.tensor_scalar(
            stats[:, 3:4], stats[:, 1:2], ccol(CB_RD), None, ALU.mult))
        P.drain("vector")
        P.op("vector", lambda v: v.tensor_tensor(
            stats[:, 4:5], stats[:, 2:3], stats[:, 2:3], ALU.mult))
        P.drain("vector")
        P.op("vector", lambda v: v.tensor_tensor(
            stats[:, 5:6], stats[:, 3:4], stats[:, 4:5], ALU.subtract))
        P.drain("vector")
        P.op("vector", lambda v: v.tensor_scalar(
            stats[:, 6:7], stats[:, 5:6], ccol(CB_EPS), None, ALU.add), "dve", 1)
        P.wait("scalar", "dve", P.cnt["dve"])
        P.op("scalar", lambda s: s.sqrt(stats[:, 7:8], stats[:, 6:7]), "act", 1)
        P.wait("vector", "act", P.cnt["act"])
        P.op("vector", lambda v: v.reciprocal(stats[:, 8:9], stats[:, 7:8]))
        P.drain("vector")
        P.op("vector", lambda v: v.tensor_scalar(
            xnbf[:], src_tile, stats[:, 2:3], stats[:, 8:9],
            ALU.subtract, ALU.mult), "dve", 1)
        P.wait("tensor", "dve", P.cnt["dve"])
        for kd in range(KD):
            P.op("tensor", lambda t, k=kd: t.transpose(
                tp_ps[:, k * 128:(k + 1) * 128],
                xnbf[:, k * 128:(k + 1) * 128], ident_bf[:]), "pe", 1)
        P.wait("vector", "pe", P.cnt["pe"])
        P.op("vector", lambda v: v.tensor_copy(
            hap(dst_sb, dst_col,
                [[dst_sb.shape[1], 128], [dst_rowlen, KD], [1, 128]]),
            tp_ps[:]), "dve", 1)

    p0fr = ExitStack()
    btmp = p0fr.enter_context(nc.sbuf_tensor("btmp", [128, D], F32))
    P.begin()
    for name, dst in [("identbf", ident_bf), ("identf", ident_f),
                      ("maskbf", mask_bf), ("mask01bf", mask01_bf),
                      ("ubiaspt", ubias_sb)]:
        P.op("sync", lambda s, d=dst, src=I[name]: s.dma_start(out=d[:], in_=src),
             "ld1", 16)
    P.op("gpsimd", lambda g: g.dma_start(out=bexp2d_sb[:], in_=I["bexp2d"]),
         "g1", 16)
    P.op("sync", lambda s: s.dma_start(
        out=hap(maskw_sb, 0, [[2048, 128], [512, 4], [1, 512]]),
        in_=hap(I["maskwbf"], 0, [[512, 128], [65536, 4], [1, 512]])), "ld1", 16)
    P.op("sync", lambda s: s.dma_start(
        out=bias3rep[:], in_=hap(I["bconrow"], 0, [[0, 128], [1, D]])), "ld1", 16)
    P.op("sync", lambda s: s.dma_start(
        out=btmp[:], in_=hap(I["dbiasrow"], 0, [[0, 128], [1, D]])), "ld1", 16)
    n_ld1 = P.cnt["ld1"]
    for col, val in [(CB_EXPB, EXP_SHIFT), (CB_EXPS, 0.125), (CB_RD, 1.0 / D),
                     (CB_EPS, 1e-5), (CB_ONE, 1.0)]:
        P.op("vector", lambda v, c=col, vv=val: v.memset(consts[:, c:c + 1], vv))
    P.wait("vector", "ld1", n_ld1)
    P.op("vector", lambda v: v.tensor_tensor(
        bias3rep[:], bias3rep[:], btmp[:], ALU.add), "dve", 1)
    P.flush("p0")
    p0fr.close()

    fr = ExitStack()
    G = 8
    matA = [fr.enter_context(nc.sbuf_tensor(f"matA{i}", [128, G * 896], BF16))
            for i in range(2)]
    wideA = [fr.enter_context(nc.sbuf_tensor(f"wideA{i}", [128, G * 1024], BF16))
             for i in range(2)]
    rp_sb = fr.enter_context(nc.sbuf_tensor("rp_sb", [128, G * 2 * 3 * 128], BF16))
    lct_sb = fr.enter_context(nc.sbuf_tensor("lct_sb", [128, G * 128], BF16))
    stg = [fr.enter_context(nc.sbuf_tensor(f"stg{i}", [128, 1024], BF16))
           for i in range(4)]
    upt_sl = [fr.enter_context(nc.sbuf_tensor(f"upt{i}", [128, 1024], BF16))
              for i in range(4)]
    ub_sb = fr.enter_context(nc.sbuf_tensor("ub_sb", [128, NC], BF16))
    bpt_sb = fr.enter_context(nc.sbuf_tensor("bpt_sb", [128, NPG], BF16))
    u_sb = fr.enter_context(nc.sbuf_tensor("u_sb", [128, NC * 1024], BF16))

    ps_all = fr.enter_context(nc.psum_tensor("ps_all", [128, 4096], F32))

    def bank(b, w=512):
        return ps_all[:, b * 512: b * 512 + w]
    ps_ap = [None] * 4

    P.begin()
    P.frame_barrier(["ld1", "g1"])
    P.wait("tensor", "ld1", n_ld1)
    P.wait("tensor", "g1", 16)

    EV = [("vector", "dve"), ("scalar", "act")]
    mat_st = Stream(P, "gm", "gpsimd")
    wide_st = Stream(P, "gw", "gpsimd")
    upt_st = Stream(P, "gu", "gpsimd", 4)
    ctw_st = Stream(P, "gc", "gpsimd", 4)
    w2w_st = Stream(P, "sw2", "sync", 4)
    w3w_st = Stream(P, "sw3", "sync", 4)

    def mat_fill(slot, src_t, grp, nk):
        mat_st.fill(slot, lambda g, s=slot, t=src_t, gg=grp, k=nk: g.dma_start(
            out=hap(matA[s], 0, [[G * 896, 128], [1, G * k * 128]]),
            in_=hap(t, gg * G * k * 128,
                    [[NC * k * 128, 128], [1, G * k * 128]])))

    def wide_fill(slot, src_t, grp):
        wide_st.fill(slot, lambda g, s=slot, t=src_t, gg=grp: g.dma_start(
            out=hap(wideA[s], 0, [[G * 1024, 128], [1, G * 1024]]),
            in_=hap(t, gg * G * 1024,
                    [[NC * 1024, 128], [1, G * 1024]])))

    PLAN_UPL = [
        [(0, ('m', 1), ('m', 0)), (1, ('m', 2), ('m', 3)), (2, ('m', 5), ('m', 4))],
        [(0, ('r', 1), ('r', 0)), (1, ('r', 2), ('m', 6))],
        [('L', ('r', 0), ('r', 1))],
    ]
    PLAN_UPH = [
        [(0, ('m', 1), ('m', 0)), (1, ('m', 2), ('m', 3))],
        [(0, ('r', 1), ('r', 0))],
        [('L', ('r', 0), ('m', 4))],
    ]
    PLAN_DNH = [
        [(0, ('m', 3), ('m', 2)), (1, ('m', 1), ('m', 0))],
        [(0, ('r', 0), ('m', 4))],
        [('L', ('r', 0), ('r', 1))],
    ]
    PLAN_DNL = [
        [(0, ('m', 1), ('m', 0)), (1, ('m', 4), ('m', 3)), (2, ('m', 5), ('m', 6))],
        [(0, ('m', 2), ('r', 0)), (1, ('r', 1), ('r', 2))],
        [('L', ('r', 1), ('r', 0))],
    ]
    RPW = 3 * 128

    def chains(mats_slot, nk, plan):
        last_d, last_a = P.cnt["dve"], P.cnt["act"]

        def spec_ap(sp, rpar):
            kind, idx = sp
            if kind == "m":
                return matA[mats_slot][:, 0:1]
            return None

        for rnd, tuples in enumerate(plan):
            P.wait("tensor", "dve", last_d)
            P.wait("tensor", "act", last_a)
            srcpar = (rnd - 1) % 2
            dstpar = rnd % 2
            nt = len(tuples)
            for g in range(G):
                for ti, (dst, lsp, rsp) in enumerate(tuples):
                    def ap_of(sp, g=g):
                        kind, idx = sp
                        if kind == "m":
                            return matA[mats_slot][:, g * nk * 128 + idx * 128:
                                                   g * nk * 128 + (idx + 1) * 128]
                        return rp_sb[:, (g * 2 + srcpar) * RPW + idx * 128:
                                     (g * 2 + srcpar) * RPW + (idx + 1) * 128]
                    out = ps_all[:, g * 512 + ti * 128: g * 512 + (ti + 1) * 128]
                    P.op("tensor", lambda t, o=out, l=ap_of(lsp), r=ap_of(rsp):
                         t.matmul(o, l, r, start=True, stop=True), "pe", 1)
            pe_here = P.cnt["pe"]
            w = nt * 128
            for half4 in range(2):
                eng, sem = EV[half4]
                P.wait(eng, "pe", pe_here)
                g0 = half4 * 4
                srcap = hap(ps_all, g0 * 512, [[4096, 128], [512, 4], [1, w]])
                if tuples[0][0] == "L":
                    dstap = hap(lct_sb, g0 * 128,
                                [[G * 128, 128], [128, 4], [1, 128]])
                else:
                    dstap = hap(rp_sb, (g0 * 2 + dstpar) * RPW,
                                [[G * 2 * RPW, 128], [2 * RPW, 4], [1, w]])
                P.op(eng, lambda e, d=dstap, s=srcap: _copy(e, d, s), sem, 1)
            last_d, last_a = P.cnt["dve"], P.cnt["act"]
        return last_d, last_a

    ap_gate = [0, 0]

    def apply_pair(gc, lhsT, rhs_lo, rhs_hi, evac_fn):
        aset = gc % 2
        eng, sem = EV[aset]
        P.wait("tensor", sem, ap_gate[aset])
        for hh, rhs in ((0, rhs_lo), (1, rhs_hi)):
            P.op("tensor", lambda t, b=bank(2 * aset + hh), l=lhsT, r=rhs:
                 t.matmul(b, l, r, start=True, stop=True), "pe", 1)
        pe_here = P.cnt["pe"]
        P.wait(eng, "pe", pe_here)
        for hh in range(2):
            evac_fn(eng, sem, hh, bank(2 * aset + hh))
        ap_gate[aset] = P.cnt[sem]

    for grp in range(4):
        ms = grp % 2
        mat_fill(ms, I["su"], grp, 7)
        wide_fill(ms, I["wexpT"], grp)
        mat_st.wait("tensor", ms)
        ld_, la_ = chains(ms, 7, PLAN_UPL)
        P.wait("tensor", "dve", ld_)
        P.wait("tensor", "act", la_)
        wide_st.wait("tensor", ms)
        for gc in range(G):
            c = grp * G + gc
            lhsT = lct_sb[:, gc * 128:(gc + 1) * 128]

            def ev(eng, sem, hh, bank, c=c):
                P.op(eng, lambda e, d=u_sb[:, c * 1024 + hh * 512:
                                           c * 1024 + (hh + 1) * 512],
                     s=bank[:]: _copy(e, d, s), sem, 1)
            apply_pair(gc, lhsT,
                       wideA[ms][:, gc * 1024: gc * 1024 + 512],
                       wideA[ms][:, gc * 1024 + 512: gc * 1024 + 1024], ev)
        P.wait("tensor", "dve", P.cnt["dve"])
        P.wait("tensor", "act", P.cnt["act"])
        for gc in range(G):
            c = grp * G + gc
            P.op("tensor", lambda t, g2=gc, cc=c: t.matmul(
                ps_all[:, 4 * 512 + g2: 4 * 512 + g2 + 1],
                lct_sb[:, g2 * 128:(g2 + 1) * 128],
                bexp2d_sb[:, cc:cc + 1], start=True, stop=True), "pe", 1)
        P.wait("vector", "pe", P.cnt["pe"])
        P.op("vector", lambda v, c0=grp * G: v.tensor_copy(
            ub_sb[:, c0:c0 + G], ps_all[:, 4 * 512: 4 * 512 + G]), "dve", 1)
        mat_st.set_gate(ms, "pe", P.cnt["pe"])
        wide_st.set_gate(ms, "pe", P.cnt["pe"])

    u_dve, u_act = P.cnt["dve"], P.cnt["act"]
    ub_done = P.cnt["dve"]

    P.wait("gpsimd", "dve", ub_done)
    for pg in range(NPG):
        bsrc = hap(ub_sb, 4 * pg * NC, [[NC, 4], [1, NC]])
        P.op("gpsimd", lambda g, s=bsrc, d=bpt_sb[:, pg:pg + 1]:
             g.dma_start(out=d, in_=s), "grl", 16)
    n_grl_b = P.cnt["grl"]

    P.wait("gpsimd", "dve", u_dve)
    P.wait("gpsimd", "act", u_act)
    P.wait("sync", "dve", u_dve)
    P.wait("sync", "act", u_act)
    upt_next = [0]

    def ensure_upt(pg_to):
        while upt_next[0] <= min(pg_to, NPG - 1):
            pg = upt_next[0]
            slot = pg % 4
            src = hap(u_sb, 4 * pg * (NC * 1024), [[NC * 1024, 4], [1, NC * 1024]])
            upt_st.fill(slot, lambda g, s=src, d=upt_sl[slot][:]:
                        g.dma_start(out=d, in_=s))
            upt_next[0] += 1

    ensure_upt(3)
    for grp in range(4):
        ms = grp % 2
        mat_fill(ms, I["tu"], grp, 5)
        mat_st.wait("tensor", ms)
        ld_, la_ = chains(ms, 5, PLAN_UPH)
        P.wait("tensor", "dve", ld_)
        P.wait("tensor", "act", la_)
        if grp == 0:
            P.wait("tensor", "grl", n_grl_b)
        for gc in range(G):
            pg = grp * G + gc
            slot = pg % 4
            aset = gc % 2
            eng, sem = EV[aset]
            upt_st.wait("tensor", slot)
            P.wait("tensor", sem, ap_gate[aset])
            ssl = pg % 4
            w2w_st.wait(eng, ssl)
            ctw_st.wait(eng, ssl)
            for kd in range(KD):
                P.op("tensor", lambda t, b=bank(2 * aset + kd // 4),
                     k=kd, s2=slot, g2=gc: t.matmul(
                         b[:, (k % 4) * 128:(k % 4 + 1) * 128],
                         upt_sl[s2][:, k * 128:(k + 1) * 128],
                         lct_sb[:, g2 * 128:(g2 + 1) * 128],
                         start=True, stop=True), "pe", 1)
            pe_here = P.cnt["pe"]
            P.wait(eng, "pe", pe_here)
            for hh in range(2):
                P.op(eng, lambda e, s2=ssl, h2=hh, b=bank(2 * aset + hh):
                     _copy(e, stg[s2][:, h2 * 512:(h2 + 1) * 512], b), sem, 1)
            ap_gate[aset] = P.cnt[sem]
            upt_st.set_gate(slot, "pe", pe_here)
            P.op("tensor", lambda t, g2=gc, p2=pg: t.matmul(
                ps_all[:, 4 * 512 + g2: 4 * 512 + g2 + 1],
                lct_sb[:, g2 * 128:(g2 + 1) * 128],
                bpt_sb[:, p2:p2 + 1], start=True, stop=True), "pe", 1)
            P.wait("sync", sem, P.cnt[sem])
            w2w_st.fill(ssl, lambda sy, p2=pg, s2=ssl: sy.dma_start(
                out=w2t_s[p2], in_=stg[s2][:]))
            ensure_upt(pg + 4)
        P.wait("vector", "pe", P.cnt["pe"])
        P.op("vector", lambda v, c0=grp * G: v.scalar_tensor_tensor(
            bias2_sb[:, c0:c0 + G], ps_all[:, 4 * 512: 4 * 512 + G], ccol(CB_ONE),
            ubias_sb[:, c0:c0 + G], ALU.mult, ALU.add), "dve", 1)
        mat_st.set_gate(ms, "pe", P.cnt["pe"])

    ct_sb = u_sb
    for sl in range(4):
        upt_st.wait("gpsimd", sl)
    for grp in range(4):
        ms = grp % 2
        mat_fill(ms, I["tdt"], grp, 5)
        wide_fill(ms, I["wconpt"], grp)
        mat_st.wait("tensor", ms)
        ld_, la_ = chains(ms, 5, PLAN_DNH)
        P.wait("tensor", "dve", ld_)
        P.wait("tensor", "act", la_)
        wide_st.wait("tensor", ms)
        for gc in range(G):
            pg = grp * G + gc
            ssl = pg % 4
            eng, sem = EV[gc % 2]
            w2w_st.wait(eng, ssl)
            ctw_st.wait(eng, ssl)
            lhsT = lct_sb[:, gc * 128:(gc + 1) * 128]

            def ev(eng2, sem2, hh, bank, ssl=ssl):
                P.op(eng2, lambda e, d=stg[ssl][:, hh * 512:(hh + 1) * 512],
                     s=bank[:]: _copy(e, d, s), sem2, 1)
            apply_pair(gc, lhsT,
                       wideA[ms][:, gc * 1024: gc * 1024 + 512],
                       wideA[ms][:, gc * 1024 + 512: gc * 1024 + 1024], ev)
            sem = EV[gc % 2][1]
            P.wait("gpsimd", sem, P.cnt[sem])
            src = hap(stg[ssl], 0, [[1024, 128], [1, 1024]])
            dst = hap(ct_sb, 4 * pg * (NC * 1024), [[NC * 1024, 4], [1, NC * 1024]])
            ctw_st.fill(ssl, lambda g, s=src, d=dst: g.dma_start(out=d, in_=s))
        mat_st.set_gate(ms, "pe", P.cnt["pe"])
        wide_st.set_gate(ms, "pe", P.cnt["pe"])

    for sl in range(4):
        ctw_st.wait("tensor", sl)
    for grp in range(4):
        ms = grp % 2
        mat_fill(ms, I["sdt"], grp, 7)
        mat_st.wait("tensor", ms)
        ld_, la_ = chains(ms, 7, PLAN_DNL)
        P.wait("tensor", "dve", ld_)
        P.wait("tensor", "act", la_)
        for gc in range(G):
            c = grp * G + gc
            ssl = c % 4
            eng, sem = EV[gc % 2]
            ctw_st.wait(eng, ssl)
            w3w_st.wait(eng, ssl)
            lhsT = lct_sb[:, gc * 128:(gc + 1) * 128]

            def ev(eng2, sem2, hh, bank, ssl=ssl):
                P.op(eng2, lambda e, d=stg[ssl][:, hh * 512:(hh + 1) * 512],
                     s=bank[:]: _copy(e, d, s), sem2, 1)
            apply_pair(gc, lhsT,
                       ct_sb[:, c * 1024: c * 1024 + 512],
                       ct_sb[:, c * 1024 + 512: c * 1024 + 1024], ev)
            sem = EV[gc % 2][1]
            P.wait("sync", sem, P.cnt[sem])
            dstap = hap(w3a_s, c * 1024,
                        [[128 * 1024, 32], [32 * 1024, 4], [1, 1024]])
            srcap = hap(stg[ssl], 0, [[1024, 128], [1, 1024]])
            w3w_st.fill(ssl, lambda sy, d=dstap, s=srcap: sy.dma_start(
                out=d, in_=s))
        mat_st.set_gate(ms, "pe", P.cnt["pe"])

    if debug:
        for st4 in (w2w_st, w3w_st):
            for sl in range(4):
                st4.wait("gpsimd", sl)
        P.op("gpsimd", lambda g: g.dma_start(out=DBG["dbg_w2t"][:], in_=w2t_s[:]),
             "sdbg", 16)
        P.op("gpsimd", lambda g: g.dma_start(out=DBG["dbg_w3a"][:], in_=w3a_s[:]),
             "sdbg", 16)
        P.wait("gpsimd", "dve", P.cnt["dve"])
        P.op("gpsimd", lambda g: g.dma_start(out=DBG["dbg_bias2"][:], in_=bias2_sb[:]),
             "sdbg", 16)
    P.flush("p1")
    fr.close()

    w2_store_cnt = [P.cnt[f"sw2{i}"] for i in range(4)]
    w3_store_cnt = [P.cnt[f"sw3{i}"] for i in range(4)]
    prep_dve = P.cnt["dve"]

    fr = ExitStack()
    xload = [fr.enter_context(nc.sbuf_tensor(f"xl{i}", [128, D], F32))
             for i in range(2)]
    xsq = fr.enter_context(nc.sbuf_tensor("xsq", [128, D], F32))
    stats = fr.enter_context(nc.sbuf_tensor("stats", [128, 16], F32))
    xnbf = fr.enter_context(nc.sbuf_tensor("xnbf", [128, D], BF16))
    xn1t = fr.enter_context(nc.sbuf_tensor("xn1t", [128, KD * T], BF16))
    wv_bf = fr.enter_context(nc.sbuf_tensor("wv_bf", [128, KD * D], BF16))
    qt_sb = fr.enter_context(nc.sbuf_tensor("qt_sb", [128, 8 * T], BF16))
    kt_sb = fr.enter_context(nc.sbuf_tensor("kt_sb", [128, 8 * T], BF16))
    VROW = NH * 65
    v_sb = fr.enter_context(nc.sbuf_tensor("v_sb", [128, TT * VROW], BF16))
    wst = [fr.enter_context(nc.sbuf_tensor(f"wst{i}", [128, KD * 128], BF16))
           for i in range(2)]
    pt_sb = fr.enter_context(nc.sbuf_tensor("pt_sb", [128, 6 * 512], BF16))
    nsb = fr.enter_context(nc.sbuf_tensor("nsb", [128, 2 * 512], F32))
    rec_sb = fr.enter_context(nc.sbuf_tensor("rec_sb", [128, 8], F32))
    stage = fr.enter_context(nc.sbuf_tensor("stage", [128, 4 * D], F32))
    r1w = [fr.enter_context(nc.sbuf_tensor(f"r1w{i}", [128, D], F32))
           for i in range(2)]

    x_st = Stream(P, "lx", "sync")
    r1_st = Stream(P, "sr1", "sync")

    pfr = ExitStack()
    tp_ps = [pfr.enter_context(nc.psum_tensor(f"tpps{i}", [128, 1024], BF16))
             for i in range(2)]
    P.begin()
    P.frame_barrier(["sw20", "sw21", "sw22", "sw23", "sw30", "sw31", "sw32",
                     "sw33", "gu0", "gu1", "gu2", "gu3", "gc0", "gc1", "gc2",
                     "gc3", "gm0", "gm1", "gw0", "gw1", "grl", "srl", "sdbg"])
    for kd in range(KD):
        P.op("gpsimd", lambda g, k=kd: g.dma_start(
            out=wv_bf[:, k * D:(k + 1) * D], in_=I["wv"][k]), "g1", 16)
    n_g1_wv = P.cnt["g1"]
    for t_i in range(TT):
        slot = t_i % 2
        x_st.fill(slot, lambda sy, ti=t_i, s=slot: sy.dma_start(
            out=xload[s][:], in_=I["x"][ti * 128:(ti + 1) * 128, :]))
        x_st.wait("scalar", slot)
        x_st.wait("vector", slot)
        ln_tile(xload[slot][:], xn1t, T, t_i * 128, xsq, stats, xnbf,
                tp_ps[t_i % 2])
        x_st.set_gate(slot, "dve", P.cnt["dve"])
    P.flush("p2")
    pfr.close()

    pfr = ExitStack()
    ps_qk = [pfr.enter_context(nc.psum_tensor(f"psqk{i}", [128, 512], F32))
             for i in range(4)]
    P.begin()
    P.op("vector", lambda v: v.memset(
        hap(v_sb, 64, [[TT * VROW, 128], [VROW, TT], [65, NH]]), 1.0), "dve", 1)
    P.wait("tensor", "g1", n_g1_wv)
    P.wait("tensor", "dve", P.cnt["dve"])
    bank_gate = [0, 0, 0, 0]
    for t_i in range(TT):
        for hh in range(2):
            bi = (t_i % 2) * 2 + hh
            bank = ps_qk[bi]
            P.wait("tensor", "dve", bank_gate[bi])
            for kd in range(KD):
                P.op("tensor", lambda t, ti=t_i, k=kd, h=hh, b=bank: t.matmul(
                    b[:], xn1t[:, k * T + ti * 128: k * T + (ti + 1) * 128],
                    wv_bf[:, k * D + h * 512: k * D + (h + 1) * 512],
                    start=(k == 0), stop=(k == KD - 1)), "pe", 1)
            P.wait("vector", "pe", P.cnt["pe"])
            P.op("vector", lambda v, ti=t_i, h=hh, b=bank: v.tensor_copy(
                hap(v_sb, ti * VROW + h * 8 * 65, [[TT * VROW, 128], [65, 8], [1, 64]]),
                hap(b, 0, [[512, 128], [64, 8], [1, 64]])), "dve", 1)
            bank_gate[bi] = P.cnt["dve"]
    wq_st = Stream(P, "gq", "gpsimd")
    for qf in range(16):
        slot = qf % 2
        wq_st.fill(slot, lambda g, q=qf, s=slot: g.dma_start(
            out=wst[s][:], in_=hap(I["wqk"], q * KD * 16384,
                                   [[128, 128], [16384, KD], [1, 128]])))
        wq_st.wait("tensor", slot)
        dst_sb = qt_sb if qf < 8 else kt_sb
        hp = qf % 8
        for ts in range(4):
            bi = slot * 2 + (ts % 2)
            bank = ps_qk[bi]
            P.wait("tensor", "dve", bank_gate[bi])
            for kd in range(KD):
                P.op("tensor", lambda t, s=slot, k=kd, tss=ts, b=bank: t.matmul(
                    b[:], wst[s][:, k * 128:(k + 1) * 128],
                    xn1t[:, k * T + tss * 512: k * T + (tss + 1) * 512],
                    start=(k == 0), stop=(k == KD - 1)), "pe", 1)
            P.wait("vector", "pe", P.cnt["pe"])
            P.op("vector", lambda v, d=dst_sb, h=hp, tss=ts, b=bank: v.tensor_copy(
                d[:, h * T + tss * 512: h * T + (tss + 1) * 512], b[:]), "dve", 1)
            bank_gate[bi] = P.cnt["dve"]
        wq_st.set_gate(slot, "pe", P.cnt["pe"])
    if debug:
        P.wait("gpsimd", "dve", P.cnt["dve"])
        P.op("gpsimd", lambda g: g.dma_start(out=DBG["dbg_xn1t"][:], in_=xn1t[:]),
             "sdbg", 16)
        P.op("gpsimd", lambda g: g.dma_start(out=DBG["dbg_qt"][:], in_=qt_sb[:]),
             "sdbg", 16)
        P.op("gpsimd", lambda g: g.dma_start(out=DBG["dbg_kt"][:], in_=kt_sb[:]),
             "sdbg", 16)
        P.op("gpsimd", lambda g: g.dma_start(out=DBG["dbg_v"][:], in_=v_sb[:]),
             "sdbg", 16)
    P.flush("p3")
    pfr.close()

    pfr = ExitStack()
    ps_s = pfr.enter_context(nc.psum_tensor("ps_s", [128, 2048], F32))
    ps_av = pfr.enter_context(nc.psum_tensor("ps_av", [128, 1024], F32))
    ps_avt = [pfr.enter_context(nc.psum_tensor(f"psavt{i}", [128, 512], F32))
              for i in range(2)]
    P.begin()
    P.wait("tensor", "dve", P.cnt["dve"])
    P.wait("scalar", "dve", P.cnt["dve"])

    exp_done = [0, 0, 0]
    pt_gate = [0, 0, 0]
    sbank_gate = [0, 0]
    mul_dve = 0
    pend_epi = None

    def emit_av(pend, hp, av_started):
        p3, jt, is_last = pend
        P.wait("tensor", "act", exp_done[p3])
        for eo in range(2):
            h = 2 * hp + eo
            P.op("tensor", lambda t, e=eo, pp=p3, j=jt, hh=h, last=is_last: t.matmul(
                ps_av[0:65, e * 512:(e + 1) * 512],
                v_sb[:, j * VROW + hh * 65: j * VROW + hh * 65 + 65],
                pt_sb[:, (pp * 2 + e) * 512:(pp * 2 + e + 1) * 512],
                start=(not av_started), stop=last), "pe", 1)

    mul_dve_box = [0]

    def emit_epilogue(epi):
        hp, pe_av = epi
        P.wait("scalar", "pe", pe_av)
        P.op("scalar", lambda s: s.copy(nsb[0:65, :], ps_av[0:65, :]), "act", 1)
        P.wait("tensor", "act", P.cnt["act"])
        P.wait("tensor", "dve", mul_dve_box[0])
        for eo in range(2):
            for t4 in range(4):
                P.op("tensor", lambda t, e=eo, q=t4: t.transpose(
                    ps_avt[e][:, q * 128:q * 128 + 65],
                    nsb[0:65, e * 512 + q * 128: e * 512 + (q + 1) * 128],
                    ident_f[0:65, 0:65]), "pe", 1)
        P.wait("vector", "pe", P.cnt["pe"])
        for eo in range(2):
            P.op("vector", lambda v, e=eo: v.reciprocal(
                hap(rec_sb, e * 4, [[8, 128], [1, 4]]),
                hap(ps_avt[e], 64, [[512, 128], [128, 4], [1, 1]])))
        P.drain("vector")
        for eo in range(2):
            h = 2 * hp + eo
            for t4 in range(4):
                P.op("vector", lambda v, e=eo, q=t4, hh=h: v.tensor_scalar(
                    stage[:, q * D + hh * 64: q * D + hh * 64 + 64],
                    ps_avt[e][:, q * 128: q * 128 + 64],
                    rec_sb[:, e * 4 + q: e * 4 + q + 1], None, ALU.mult),
                    "dve", 1)
        mul_dve_box[0] = P.cnt["dve"]

    for ts in range(4):
        jmax = 4 * ts + 4
        for hp in range(8):
            av_started = False
            pends = []
            for jt in range(jmax):
                r = jt % 2
                p3 = jt % 3
                P.wait("tensor", "act", sbank_gate[r])
                diag = jt >= 4 * ts
                for eo in range(2):
                    P.op("tensor", lambda t, e=eo, j=jt, h=hp, tss=ts, rr=r, dg=diag:
                         t.matmul(
                             ps_s[:, (rr * 2 + e) * 512:(rr * 2 + e + 1) * 512],
                             kt_sb[e * 64:(e + 1) * 64,
                                   h * T + j * 128: h * T + (j + 1) * 128],
                             qt_sb[e * 64:(e + 1) * 64,
                                   h * T + tss * 512: h * T + (tss + 1) * 512],
                             start=True, stop=not dg,
                             tile_position=(e * 64, 0)), "pe", 1)
                if diag:
                    off4 = jt - 4 * ts
                    for eo in range(2):
                        P.op("tensor", lambda t, e=eo, rr=r, o4=off4: t.matmul(
                            ps_s[:, (rr * 2 + e) * 512:(rr * 2 + e + 1) * 512],
                            ident_bf[:], maskw_sb[:, o4 * 512:(o4 + 1) * 512],
                            start=False, stop=True), "pe", 1)
                pe_scores = P.cnt["pe"]
                if jt == 0 and pend_epi is not None:
                    emit_epilogue(pend_epi)
                    pend_epi = None
                if len(pends) >= 2:
                    emit_av(pends.pop(0), hp, av_started)
                    av_started = True
                    pt_gate[(jt - 2) % 3] = P.cnt["pe"]
                P.wait("scalar", "pe", pe_scores)
                P.wait("scalar", "pe", pt_gate[p3])
                P.op("scalar", lambda s, rr=r, pp=p3: s.activation(
                    pt_sb[:, pp * 1024:(pp + 1) * 1024],
                    ps_s[:, rr * 1024:(rr + 1) * 1024], AF.Exp,
                    bias=ccol(CB_EXPB), scale=ccol(CB_EXPS)), "act", 1)
                exp_done[p3] = P.cnt["act"]
                sbank_gate[r] = P.cnt["act"]
                pends.append((p3, jt, jt == jmax - 1))
            for pend in pends:
                emit_av(pend, hp, av_started)
                av_started = True
                pt_gate[pend[0]] = P.cnt["pe"]
            pends = []
            pend_epi = (hp, P.cnt["pe"])
        emit_epilogue(pend_epi)
        pend_epi = None
        for t4 in range(4):
            t_i = ts * 4 + t4
            slot = t_i % 2
            x_st.fill(slot, lambda sy, ti=t_i, s=slot: sy.dma_start(
                out=xload[s][:], in_=I["x"][ti * 128:(ti + 1) * 128, :]))
            x_st.wait("vector", slot)
            r1_st.wait("vector", slot)
            P.drain("vector")
            P.op("vector", lambda v, q=t4, s=slot: v.tensor_tensor(
                r1w[s][:], stage[:, q * D:(q + 1) * D], xload[s][:], ALU.add),
                "dve", 1)
            x_st.set_gate(slot, "dve", P.cnt["dve"])
            P.wait("sync", "dve", P.cnt["dve"])
            r1_st.fill(slot, lambda sy, ti=t_i, s=slot: sy.dma_start(
                out=r1_s[ti * 128:(ti + 1) * 128, :], in_=r1w[s][:]))
        P.drain("vector")
    if debug:
        r1_st.wait("gpsimd", 0)
        r1_st.wait("gpsimd", 1)
        P.op("gpsimd", lambda g: g.dma_start(out=DBG["dbg_r1"][:], in_=r1_s[:]),
             "sdbg", 16)
    P.flush("p4")
    pfr.close()
    fr.close()

    fr = ExitStack()
    r1h = fr.enter_context(nc.sbuf_tensor("r1h", [128, 8 * D], F32))
    xsq2 = fr.enter_context(nc.sbuf_tensor("xsq2", [128, D], F32))
    stats2 = fr.enter_context(nc.sbuf_tensor("stats2", [128, 16], F32))
    xnbf2 = fr.enter_context(nc.sbuf_tensor("xnbf2", [128, D], BF16))
    xn2t = fr.enter_context(nc.sbuf_tensor("xn2t", [128, KD * 1024], BF16))
    h4 = fr.enter_context(nc.sbuf_tensor("h4", [128, NC * 1024], BF16))
    w2sb = [fr.enter_context(nc.sbuf_tensor(f"w2sb{i}", [128, 1024], BF16))
            for i in range(2)]
    w3sb = [fr.enter_context(nc.sbuf_tensor(f"w3sb{i}", [128, 1024], BF16))
            for i in range(2)]
    outst = [fr.enter_context(nc.sbuf_tensor(f"outst{i}", [128, D], F32))
             for i in range(2)]
    w3res = fr.enter_context(nc.sbuf_tensor("w3res", [128, 24 * 1024], BF16))

    w2r_st = Stream(P, "lw2", "sync")
    w3r_st = Stream(P, "lw3", "sync")
    y_st = Stream(P, "sy", "sync")
    r1l_gate = [0, 0]

    for half in range(2):
        pfr = ExitStack()
        tp2 = [pfr.enter_context(nc.psum_tensor(f"tp2_{half}_{i}", [128, 1024], BF16))
               for i in range(2)]
        P.begin()
        if half == 0:
            P.frame_barrier(["sr10", "sr11", "lx0", "lx1", "gq0", "gq1",
                             "g1", "sdbg"])
            if debug:
                for e in ENGINES:
                    P.wait(e, "sdbg", P.cnt["sdbg"])
            for i4 in range(4):
                P.wait("sync", f"sw2{i4}", w2_store_cnt[i4])
                P.wait("sync", f"sw3{i4}", w3_store_cnt[i4])
            P.wait("sync", "sr10", P.cnt["sr10"])
            P.wait("sync", "sr11", P.cnt["sr11"])
            P.sem("sw3r")
            for i4 in range(4):
                P.wait("scalar", f"sw3{i4}", w3_store_cnt[i4])
            for p in range(24):
                P.op("scalar", lambda s, pp=p: s.dma_start(
                    out=w3res[:, pp * 1024:(pp + 1) * 1024],
                    in_=w3a_s[pp]), "sw3r", 16)
        else:
            P.wait("vector", "pe", P.cnt["pe"])
        for tl in range(8):
            t_i = half * 8 + tl
            slot = tl % 2
            P.gwait("sync", ("dve", r1l_gate[slot]) if r1l_gate[slot] else None)
            x_st.fill(slot, lambda sy, ti=t_i, l=tl: sy.dma_start(
                out=r1h[:, l * D:(l + 1) * D], in_=r1_s[ti * 128:(ti + 1) * 128, :]))
            x_st.wait("scalar", slot)
            x_st.wait("vector", slot)
            r1tile = r1h[:, tl * D:(tl + 1) * D]
            ln_tile(r1tile, xn2t, 1024, tl * 128, xsq2, stats2, xnbf2, tp2[tl % 2])
            x_st.set_gate(slot, "dve", P.cnt["dve"])
            P.drain("vector")
            P.op("vector", lambda v, rt=r1tile: v.tensor_tensor(
                rt, rt, bias3rep[:], ALU.add), "dve", 1)
        P.flush(f"p5a{half}")
        pfr.close()

        pfr = ExitStack()
        ps_up = [pfr.enter_context(nc.psum_tensor(f"psup_{half}_{i}", [128, 512], F32))
                 for i in range(2)]
        P.begin()
        if debug and half == 1:
            P.wait("scalar", "sdbg", P.cnt["sdbg"])
        P.wait("tensor", "dve", P.cnt["dve"])
        P.wait("scalar", "dve", prep_dve)
        up_gate = [0, 0]
        for pg in range(NPG):
            slot = pg % 2
            w2r_st.fill(slot, lambda sy, p=pg, s=slot: sy.dma_start(
                out=w2sb[s][:], in_=w2t_s[p]))
            w2r_st.wait("tensor", slot)
            for ts2 in range(2):
                bank = ps_up[ts2]
                P.wait("tensor", "act", up_gate[ts2])
                for kd in range(KD):
                    P.op("tensor", lambda t, s=slot, k=kd, t2=ts2, b=bank: t.matmul(
                        b[:], w2sb[s][:, k * 128:(k + 1) * 128],
                        xn2t[:, k * 1024 + t2 * 512: k * 1024 + (t2 + 1) * 512],
                        start=(k == 0), stop=(k == KD - 1)), "pe", 1)
                P.wait("scalar", "pe", P.cnt["pe"])
                P.op("scalar", lambda s, p=pg, t2=ts2, b=bank: s.activation(
                    h4[:, p * 1024 + t2 * 512: p * 1024 + (t2 + 1) * 512],
                    b[:], AF.Gelu, bias=bias2_sb[:, p:p + 1]), "act", 1)
                up_gate[ts2] = P.cnt["act"]
            w2r_st.set_gate(slot, "pe", P.cnt["pe"])
        P.flush(f"p5b{half}")
        pfr.close()

        pfr = ExitStack()
        ps_dn = [pfr.enter_context(nc.psum_tensor(f"psdn_{half}_{i}", [128, 512], F32))
                 for i in range(8)]
        P.begin()
        P.wait("tensor", "act", P.cnt["act"])
        P.wait("tensor", "sw3r", P.cnt["sw3r"])
        dn_gates = [0] * 8
        for ttg in range(2):
            for pg in range(NPG):
                if pg < 24:
                    wsrc, woff = w3res, pg * 1024
                else:
                    slot = pg % 2
                    w3r_st.fill(slot, lambda sy, p=pg, s=slot: sy.dma_start(
                        out=w3sb[s][:], in_=w3a_s[p]))
                    w3r_st.wait("tensor", slot)
                    wsrc, woff = w3sb[slot], 0
                for tt in range(4):
                    tloc = ttg * 4 + tt
                    for ds in range(2):
                        bank = ps_dn[tt * 2 + ds]
                        if pg == 0:
                            P.wait("tensor", "dve", dn_gates[tt * 2 + ds])
                        P.op("tensor",
                             lambda t, p=pg, w=wsrc, wo=woff, tl=tloc, d=ds,
                             b=bank: t.matmul(
                                 b[:],
                                 h4[:, p * 1024 + tl * 128: p * 1024 + (tl + 1) * 128],
                                 w[:, wo + d * 512: wo + (d + 1) * 512],
                                 start=(p == 0), stop=(p == NPG - 1)), "pe", 1)
                if pg >= 24:
                    w3r_st.set_gate(slot, "pe", P.cnt["pe"])
            pe_dn = P.cnt["pe"]
            P.wait("vector", "pe", pe_dn)
            for tt in range(4):
                tloc = ttg * 4 + tt
                t_i = half * 8 + tloc
                oslot = tloc % 2
                y_st.wait("vector", oslot)
                for ds in range(2):
                    P.op("vector",
                         lambda v, tl=tloc, d=ds, o=oslot, b=ps_dn[tt * 2 + ds]:
                         v.scalar_tensor_tensor(
                             outst[o][:, d * 512:(d + 1) * 512], b[:], ccol(CB_ONE),
                             r1h[:, tl * D + d * 512: tl * D + d * 512 + 512],
                             ALU.mult, ALU.add), "dve", 1)
                    dn_gates[tt * 2 + ds] = P.cnt["dve"]
                P.wait("sync", "dve", P.cnt["dve"])
                y_st.fill(oslot, lambda sy, ti=t_i, o=oslot: sy.dma_start(
                    out=y[ti * 128:(ti + 1) * 128, :], in_=outst[o][:]))
        r1l_gate = [P.cnt["dve"], P.cnt["dve"]]
        if debug and half == 0:
            P.wait("gpsimd", "act", P.cnt["act"])
            P.op("gpsimd", lambda g: g.dma_start(out=DBG["dbg_h4"][:], in_=h4[:]),
                 "sdbg", 16)
        P.flush(f"p5c{half}")
        pfr.close()

    P.begin()
    P.wait("sync", "sy0", P.cnt["sy0"])
    P.wait("sync", "sy1", P.cnt["sy1"])
    if debug:
        P.wait("gpsimd", "sdbg", P.cnt["sdbg"])
    P.flush("pfin")
    fr.close()
    st.close()
    return nc


def kernel(L0_attn_x, L0_attn_W_qkv, L0_ffn_W_exp, L0_ffn_b_exp,
           L0_ffn_up_weights, L0_ffn_up_bias, L0_ffn_down_weights,
           L0_ffn_W_con, L0_ffn_b_con, L0_ffn_down_bias):
    x = np.ascontiguousarray(np.asarray(L0_attn_x, dtype=np.float32))
    host = _host_arrays(np.asarray(L0_attn_W_qkv, np.float32),
                        np.asarray(L0_ffn_W_exp, np.float32),
                        np.asarray(L0_ffn_b_exp, np.float32),
                        np.asarray(L0_ffn_up_weights, np.float32),
                        np.asarray(L0_ffn_up_bias, np.float32),
                        np.asarray(L0_ffn_down_weights, np.float32),
                        np.asarray(L0_ffn_W_con, np.float32),
                        np.asarray(L0_ffn_b_con, np.float32),
                        np.asarray(L0_ffn_down_bias, np.float32))
    nc = build_nc()
    in_maps = [dict(host, x=np.ascontiguousarray(x[b])) for b in range(8)]
    res = run_bass_kernel_spmd(nc, in_maps, core_ids=list(range(8)))
    return np.stack([res.results[b]["y"] for b in range(8)], axis=0)


# version 22 (final)
# speedup vs baseline: 1.0154x; 1.0154x over previous
import numpy as np
import ml_dtypes
from contextlib import ExitStack
import concourse.bass as bass
import concourse.mybir as mybir
from concourse.bass_utils import run_bass_kernel_spmd

F32 = mybir.dt.float32
BF16 = mybir.dt.bfloat16
AF = mybir.ActivationFunctionType
ALU = mybir.AluOpType
bfdt = ml_dtypes.bfloat16
AX = mybir.AxisListType.X

T = 2048
D = 1024
NH = 16
TT = T // 128
KD = D // 128
NC = 32
NPG = 32
EXP_SHIFT = -8.0

ENGINES = ("sync", "tensor", "vector", "scalar", "gpsimd")
HD = 4096


def _stage_vectors(weights_k, s):
    nb = HD // (2 * s)
    w = weights_k.reshape(nb, s, 2, 2)
    d0 = np.zeros(HD, dtype=np.float32)
    od = np.zeros(HD, dtype=np.float32)
    i = (np.arange(nb)[:, None] * 2 * s + np.arange(s)[None, :]).ravel()
    j = i + s
    d0[i] = w[:, :, 0, 0].ravel()
    d0[j] = w[:, :, 1, 1].ravel()
    od[i] = w[:, :, 0, 1].ravel()
    od[j] = w[:, :, 1, 0].ravel()
    return d0, od


def _stage_mats(weights):
    S_low = np.zeros((7, NC, 128, 128), dtype=np.float32)
    T_blk = np.zeros((5, NPG, 128, 128), dtype=np.float32)
    ar = np.arange(128)
    for k in range(7):
        s = 1 << k
        d0, od = _stage_vectors(weights[k], s)
        for c in range(NC):
            M = S_low[k, c]
            M[ar, ar] = d0[c * 128 + ar]
            M[ar, ar ^ s] = od[c * 128 + ar]
    p_lo = ar >> 5
    c_arr = ar & 31
    for k2 in range(5):
        s = 1 << (7 + k2)
        sh = s >> 7
        d0, od = _stage_vectors(weights[7 + k2], s)
        for pg in range(NPG):
            p = 4 * pg + p_lo
            f = c_arr * 128 + p
            M = T_blk[k2, pg]
            M[ar, ar] = d0[f]
            M[ar, ar ^ sh] = od[f]
    return S_low, T_blk


def _host_arrays(W_qkv, W_exp, b_exp, up_w, up_b, down_w, W_con, b_con, down_b):
    h = {}
    Wqk = W_qkv[:, 0:2 * D]
    h["wqk"] = np.ascontiguousarray(
        Wqk.reshape(KD, 128, 16, 128).transpose(2, 0, 1, 3))
    h["wv"] = np.ascontiguousarray(W_qkv[:, 2 * D:3 * D].reshape(KD, 128, D))
    h["wexpT"] = np.ascontiguousarray(
        W_exp.T.reshape(NC, 128, D).transpose(1, 0, 2))
    h["wconpt"] = np.ascontiguousarray(
        W_con.reshape(NC, NPG, 4, D).transpose(1, 2, 0, 3).reshape(NPG, 128, D)
        .transpose(1, 0, 2))
    S_u, T_u = _stage_mats(up_w)
    S_d, T_d = _stage_mats(down_w)
    def _mix(S, tset):
        out = S.copy()
        for k in tset:
            out[k] = S[k].transpose(0, 2, 1)
        return np.ascontiguousarray(out.transpose(2, 1, 0, 3))
    h["su"] = _mix(S_u, (1, 3, 5, 6))
    h["tu"] = _mix(T_u, (1, 3, 4))
    h["sdt"] = _mix(S_d, (1, 2, 4, 6))
    h["tdt"] = _mix(T_d, (1, 3, 4))
    h["bexp2d"] = np.ascontiguousarray(b_exp.reshape(NC, 128).T)
    h["ubiaspt"] = np.ascontiguousarray(
        up_b.reshape(NC, NPG, 4).transpose(2, 0, 1).reshape(128, NPG))
    h["bconrow"] = np.ascontiguousarray(b_con.reshape(1, D))
    h["dbiasrow"] = np.ascontiguousarray(down_b.reshape(1, D))
    h["identbf"] = np.eye(128, dtype=bfdt)
    h["identf"] = np.eye(128, dtype=np.float32)
    ar = np.arange(128)
    h["maskbf"] = np.where(ar[:, None] > ar[None, :], -1e5, 0.0).astype(bfdt)
    h["mask01bf"] = (ar[:, None] <= ar[None, :]).astype(bfdt)
    mw = np.zeros((4, 128, 512), dtype=np.float32)
    for o in range(4):
        mw[o, :, :o * 128] = -1e5
        mw[o, :, o * 128:(o + 1) * 128] = np.where(
            ar[:, None] > ar[None, :], -1e5, 0.0)
    h["maskwbf"] = mw.astype(bfdt)
    return h


class Prog:

    def __init__(self, nc, stack):
        self.nc = nc
        self.stack = stack
        self.sems = {}
        self.cnt = {}
        self.waited = {}
        self.streams = None

    def sem(self, name):
        if name not in self.sems:
            self.sems[name] = self.stack.enter_context(self.nc.semaphore(name))
            self.cnt[name] = 0
        return name

    def begin(self):
        self.streams = {e: [] for e in ENGINES}

    def wait(self, engine, sname, val):
        if val <= 0:
            return
        key = (engine, sname)
        if self.waited.get(key, 0) >= val:
            return
        self.waited[key] = val
        h = self.sems[sname]
        self.streams[engine].append(lambda eng, hh=h, v=val: eng.wait_ge(hh, v))

    def gwait(self, engine, gate):
        if gate is not None:
            self.wait(engine, gate[0], gate[1])

    def op(self, engine, builder, inc=None, n=1):
        if inc is not None:
            self.cnt[inc] += n
            h = self.sems[inc]
            self.streams[engine].append(
                lambda eng, b=builder, hh=h, m=n: b(eng).then_inc(hh, m))
            return self.cnt[inc]
        self.streams[engine].append(lambda eng, b=builder: b(eng))
        return None

    def drain(self, engine):
        self.streams[engine].append(lambda eng: eng.drain())

    def frame_barrier(self, sems):
        for e in ENGINES:
            for s in sems:
                if s in self.cnt:
                    self.wait(e, s, self.cnt[s])
            for s in ("dve", "act", "pe"):
                self.wait(e, s, self.cnt[s])
        for e in ("vector", "scalar"):
            self.drain(e)

    def flush(self, name):
        streams = self.streams
        self.streams = None
        with self.nc.Block(name) as block:
            for ename in ENGINES:
                ops = streams[ename]
                if not ops:
                    continue

                def body(eng, ops=ops):
                    for f in ops:
                        f(eng)
                getattr(block, ename)(body)


class Stream:

    def __init__(self, P, name, queue, nslots=2):
        self.P = P
        self.name = name
        self.queue = queue
        self.sems = [P.sem(f"{name}{i}") for i in range(nslots)]
        self.fills = [0] * nslots
        self.gates = [None] * nslots

    def fill(self, slot, builder):
        self.P.gwait(self.queue, self.gates[slot])
        self.P.op(self.queue, builder, self.sems[slot], 16)
        self.fills[slot] += 1

    def wait(self, engine, slot):
        self.P.wait(engine, self.sems[slot], 16 * self.fills[slot])

    def set_gate(self, slot, sname, val):
        self.gates[slot] = (sname, val)


def hap(t, offset, dims):
    th = t.tensor if isinstance(t, bass.AP) else t
    return bass.AP(th, offset, dims)


def _copy(eng, dst, src):
    if hasattr(eng, "tensor_copy"):
        return eng.tensor_copy(dst, src)
    return eng.copy(dst, src)


def build_nc(debug=False, prep_only=False, secmask=0xFFFF):
    nc = bass.Bass("TRN2", target_bir_lowering=False)
    st = ExitStack()
    P = Prog(nc, st)

    I = {}
    I["x"] = nc.dram_tensor("x", [T, D], F32, kind="ExternalInput").ap()
    I["wqk"] = nc.dram_tensor("wqk", [16, KD, 128, 128], F32, kind="ExternalInput").ap()
    I["wv"] = nc.dram_tensor("wv", [KD, 128, D], F32, kind="ExternalInput").ap()
    I["wexpT"] = nc.dram_tensor("wexpT", [128, NC, D], F32, kind="ExternalInput").ap()
    I["wconpt"] = nc.dram_tensor("wconpt", [128, NPG, D], F32, kind="ExternalInput").ap()
    I["su"] = nc.dram_tensor("su", [128, NC, 7, 128], F32, kind="ExternalInput").ap()
    I["tu"] = nc.dram_tensor("tu", [128, NPG, 5, 128], F32, kind="ExternalInput").ap()
    I["sdt"] = nc.dram_tensor("sdt", [128, NC, 7, 128], F32, kind="ExternalInput").ap()
    I["tdt"] = nc.dram_tensor("tdt", [128, NPG, 5, 128], F32, kind="ExternalInput").ap()
    I["bexp2d"] = nc.dram_tensor("bexp2d", [128, NC], F32, kind="ExternalInput").ap()
    I["ubiaspt"] = nc.dram_tensor("ubiaspt", [128, NPG], F32, kind="ExternalInput").ap()
    I["bconrow"] = nc.dram_tensor("bconrow", [1, D], F32, kind="ExternalInput").ap()
    I["dbiasrow"] = nc.dram_tensor("dbiasrow", [1, D], F32, kind="ExternalInput").ap()
    I["identbf"] = nc.dram_tensor("identbf", [128, 128], BF16, kind="ExternalInput").ap()
    I["identf"] = nc.dram_tensor("identf", [128, 128], F32, kind="ExternalInput").ap()
    I["maskbf"] = nc.dram_tensor("maskbf", [128, 128], BF16, kind="ExternalInput").ap()
    I["mask01bf"] = nc.dram_tensor("mask01bf", [128, 128], BF16, kind="ExternalInput").ap()
    I["maskwbf"] = nc.dram_tensor("maskwbf", [4, 128, 512], BF16, kind="ExternalInput").ap()

    w2t_s = nc.dram_tensor("w2t_s", [NPG, 128, KD * 128], BF16, kind="Internal").ap()
    w3a_s = nc.dram_tensor("w3a_s", [NPG, 128, D], BF16, kind="Internal").ap()
    r1_s = nc.dram_tensor("r1_s", [T, D], F32, kind="Internal").ap()
    y = nc.dram_tensor("y", [T, D], F32, kind="ExternalOutput").ap()

    DBG = {}
    if debug:
        for nm, shape, dt in [("dbg_xn1t", [128, KD * T], BF16),
                              ("dbg_qt", [128, 8 * T], BF16),
                              ("dbg_kt", [128, 8 * T], BF16),
                              ("dbg_v", [128, TT * NH * 65], BF16),
                              ("dbg_r1", [T, D], F32),
                              ("dbg_w2t", [NPG, 128, KD * 128], BF16),
                              ("dbg_w3a", [NPG, 128, D], BF16),
                              ("dbg_bias2", [128, NPG], F32),
                              ("dbg_h4", [128, NC * 1024], BF16)]:
            DBG[nm] = nc.dram_tensor(nm, shape, dt, kind="ExternalOutput").ap()

    for s in ("pe", "act", "dve", "ld1", "g1", "grl", "sdbg"):
        P.sem(s)

    ident_bf = st.enter_context(nc.sbuf_tensor("ident_bf", [128, 128], BF16))
    ident_f = st.enter_context(nc.sbuf_tensor("ident_f", [128, 128], F32))
    mask_bf = st.enter_context(nc.sbuf_tensor("mask_bf", [128, 128], BF16))
    mask01_bf = st.enter_context(nc.sbuf_tensor("mask01_bf", [128, 128], BF16))
    maskw_sb = st.enter_context(nc.sbuf_tensor("maskw_sb", [128, 4 * 512], BF16))
    consts = st.enter_context(nc.sbuf_tensor("consts", [128, 8], F32))
    CB_EXPB, CB_EXPS, CB_RD, CB_EPS, CB_ONE = 0, 1, 2, 3, 4
    bias3rep = st.enter_context(nc.sbuf_tensor("bias3rep", [128, D], F32))
    bias2_sb = st.enter_context(nc.sbuf_tensor("bias2_sb", [128, NPG], F32))
    bexp2d_sb = st.enter_context(nc.sbuf_tensor("bexp2d_sb", [128, NC], BF16))
    ubias_sb = st.enter_context(nc.sbuf_tensor("ubias_sb", [128, NPG], F32))

    def ccol(i):
        return consts[:, i:i + 1]

    def ln_tile(src_tile, dst_sb, dst_rowlen, dst_col, xsq, stats, xnbf, tp_ps):
        P.op("scalar", lambda s: s.square(xsq[:], src_tile), "act", 1)
        P.op("vector", lambda v: v.reduce_sum(stats[:, 0:1], src_tile, AX))
        P.wait("vector", "act", P.cnt["act"])
        P.op("vector", lambda v: v.reduce_sum(stats[:, 1:2], xsq[:], AX))
        P.drain("vector")
        P.op("vector", lambda v: v.tensor_scalar(
            stats[:, 2:3], stats[:, 0:1], ccol(CB_RD), None, ALU.mult))
        P.op("vector", lambda v: v.tensor_scalar(
            stats[:, 3:4], stats[:, 1:2], ccol(CB_RD), None, ALU.mult))
        P.drain("vector")
        P.op("vector", lambda v: v.tensor_tensor(
            stats[:, 4:5], stats[:, 2:3], stats[:, 2:3], ALU.mult))
        P.drain("vector")
        P.op("vector", lambda v: v.tensor_tensor(
            stats[:, 5:6], stats[:, 3:4], stats[:, 4:5], ALU.subtract))
        P.drain("vector")
        P.op("vector", lambda v: v.tensor_scalar(
            stats[:, 6:7], stats[:, 5:6], ccol(CB_EPS), None, ALU.add), "dve", 1)
        P.wait("scalar", "dve", P.cnt["dve"])
        P.op("scalar", lambda s: s.sqrt(stats[:, 7:8], stats[:, 6:7]), "act", 1)
        P.wait("vector", "act", P.cnt["act"])
        P.op("vector", lambda v: v.reciprocal(stats[:, 8:9], stats[:, 7:8]))
        P.drain("vector")
        P.op("vector", lambda v: v.tensor_scalar(
            xnbf[:], src_tile, stats[:, 2:3], stats[:, 8:9],
            ALU.subtract, ALU.mult), "dve", 1)
        P.wait("tensor", "dve", P.cnt["dve"])
        for kd in range(KD):
            P.op("tensor", lambda t, k=kd: t.transpose(
                tp_ps[:, k * 128:(k + 1) * 128],
                xnbf[:, k * 128:(k + 1) * 128], ident_bf[:]), "pe", 1)
        P.wait("vector", "pe", P.cnt["pe"])
        P.op("vector", lambda v: v.tensor_copy(
            hap(dst_sb, dst_col,
                [[dst_sb.shape[1], 128], [dst_rowlen, KD], [1, 128]]),
            tp_ps[:]), "dve", 1)

    p0fr = ExitStack()
    btmp = p0fr.enter_context(nc.sbuf_tensor("btmp", [128, D], F32))
    P.begin()
    for name, dst in [("identbf", ident_bf), ("identf", ident_f),
                      ("maskbf", mask_bf), ("mask01bf", mask01_bf),
                      ("ubiaspt", ubias_sb)]:
        P.op("sync", lambda s, d=dst, src=I[name]: s.dma_start(out=d[:], in_=src),
             "ld1", 16)
    P.op("gpsimd", lambda g: g.dma_start(out=bexp2d_sb[:], in_=I["bexp2d"]),
         "g1", 16)
    P.op("sync", lambda s: s.dma_start(
        out=hap(maskw_sb, 0, [[2048, 128], [512, 4], [1, 512]]),
        in_=hap(I["maskwbf"], 0, [[512, 128], [65536, 4], [1, 512]])), "ld1", 16)
    P.op("sync", lambda s: s.dma_start(
        out=bias3rep[:], in_=hap(I["bconrow"], 0, [[0, 128], [1, D]])), "ld1", 16)
    P.op("sync", lambda s: s.dma_start(
        out=btmp[:], in_=hap(I["dbiasrow"], 0, [[0, 128], [1, D]])), "ld1", 16)
    n_ld1 = P.cnt["ld1"]
    for col, val in [(CB_EXPB, EXP_SHIFT), (CB_EXPS, 0.125), (CB_RD, 1.0 / D),
                     (CB_EPS, 1e-5), (CB_ONE, 1.0)]:
        P.op("vector", lambda v, c=col, vv=val: v.memset(consts[:, c:c + 1], vv))
    P.wait("vector", "ld1", n_ld1)
    P.op("vector", lambda v: v.tensor_tensor(
        bias3rep[:], bias3rep[:], btmp[:], ALU.add), "dve", 1)
    P.flush("p0")
    p0fr.close()

    fr = ExitStack()
    G = 8
    matA = [fr.enter_context(nc.sbuf_tensor(f"matA{i}", [128, G * 896], BF16))
            for i in range(2)]
    wideA = [fr.enter_context(nc.sbuf_tensor(f"wideA{i}", [128, G * 1024], BF16))
             for i in range(2)]
    rp_sb = fr.enter_context(nc.sbuf_tensor("rp_sb", [128, G * 2 * 3 * 128], BF16))
    lct_sb = fr.enter_context(nc.sbuf_tensor("lct_sb", [128, G * 128], BF16))
    stg = [fr.enter_context(nc.sbuf_tensor(f"stg{i}", [128, 1024], BF16))
           for i in range(4)]
    upt_sl = [fr.enter_context(nc.sbuf_tensor(f"upt{i}", [128, 1024], BF16))
              for i in range(4)]
    ub_sb = fr.enter_context(nc.sbuf_tensor("ub_sb", [128, NC], BF16))
    bpt_sb = fr.enter_context(nc.sbuf_tensor("bpt_sb", [128, NPG], BF16))
    u_sb = fr.enter_context(nc.sbuf_tensor("u_sb", [128, NC * 1024], BF16))

    ps_all = fr.enter_context(nc.psum_tensor("ps_all", [128, 4096], F32))

    def bank(b, w=512):
        return ps_all[:, b * 512: b * 512 + w]
    ps_ap = [None] * 4

    P.begin()
    P.frame_barrier(["ld1", "g1"])
    P.wait("tensor", "ld1", n_ld1)
    P.wait("tensor", "g1", 16)

    EV = [("vector", "dve"), ("scalar", "act")]
    mat_st = Stream(P, "gm", "gpsimd")
    wide_st = Stream(P, "gw", "gpsimd")
    upt_st = Stream(P, "gu", "gpsimd", 4)
    ctw_st = Stream(P, "gc", "gpsimd", 4)
    w2w_st = Stream(P, "sw2", "sync", 4)
    w3w_st = Stream(P, "sw3", "sync", 4)

    def mat_fill(slot, src_t, grp, nk):
        mat_st.fill(slot, lambda g, s=slot, t=src_t, gg=grp, k=nk: g.dma_start(
            out=hap(matA[s], 0, [[G * 896, 128], [1, G * k * 128]]),
            in_=hap(t, gg * G * k * 128,
                    [[NC * k * 128, 128], [1, G * k * 128]])))

    def wide_fill(slot, src_t, grp):
        wide_st.fill(slot, lambda g, s=slot, t=src_t, gg=grp: g.dma_start(
            out=hap(wideA[s], 0, [[G * 1024, 128], [1, G * 1024]]),
            in_=hap(t, gg * G * 1024,
                    [[NC * 1024, 128], [1, G * 1024]])))

    PLAN_UPL = [
        [(0, ('m', 1), ('m', 0)), (1, ('m', 2), ('m', 3)), (2, ('m', 5), ('m', 4))],
        [(0, ('r', 1), ('r', 0)), (1, ('r', 2), ('m', 6))],
        [('L', ('r', 0), ('r', 1))],
    ]
    PLAN_UPH = [
        [(0, ('m', 1), ('m', 0)), (1, ('m', 2), ('m', 3))],
        [(0, ('r', 1), ('r', 0))],
        [('L', ('r', 0), ('m', 4))],
    ]
    PLAN_DNH = [
        [(0, ('m', 3), ('m', 2)), (1, ('m', 1), ('m', 0))],
        [(0, ('r', 0), ('m', 4))],
        [('L', ('r', 0), ('r', 1))],
    ]
    PLAN_DNL = [
        [(0, ('m', 1), ('m', 0)), (1, ('m', 4), ('m', 3)), (2, ('m', 5), ('m', 6))],
        [(0, ('m', 2), ('r', 0)), (1, ('r', 1), ('r', 2))],
        [('L', ('r', 1), ('r', 0))],
    ]
    RPW = 3 * 128

    def chains(mats_slot, nk, plan):
        last_d, last_a = P.cnt["dve"], P.cnt["act"]

        def spec_ap(sp, rpar):
            kind, idx = sp
            if kind == "m":
                return matA[mats_slot][:, 0:1]
            return None

        for rnd, tuples in enumerate(plan):
            P.wait("tensor", "dve", last_d)
            P.wait("tensor", "act", last_a)
            srcpar = (rnd - 1) % 2
            dstpar = rnd % 2
            nt = len(tuples)
            for g in range(G):
                for ti, (dst, lsp, rsp) in enumerate(tuples):
                    def ap_of(sp, g=g):
                        kind, idx = sp
                        if kind == "m":
                            return matA[mats_slot][:, g * nk * 128 + idx * 128:
                                                   g * nk * 128 + (idx + 1) * 128]
                        return rp_sb[:, (g * 2 + srcpar) * RPW + idx * 128:
                                     (g * 2 + srcpar) * RPW + (idx + 1) * 128]
                    out = ps_all[:, g * 512 + ti * 128: g * 512 + (ti + 1) * 128]
                    P.op("tensor", lambda t, o=out, l=ap_of(lsp), r=ap_of(rsp):
                         t.matmul(o, l, r, start=True, stop=True), "pe", 1)
            pe_here = P.cnt["pe"]
            w = nt * 128
            for half4 in range(2):
                eng, sem = EV[half4]
                P.wait(eng, "pe", pe_here)
                g0 = half4 * 4
                srcap = hap(ps_all, g0 * 512, [[4096, 128], [512, 4], [1, w]])
                if tuples[0][0] == "L":
                    dstap = hap(lct_sb, g0 * 128,
                                [[G * 128, 128], [128, 4], [1, 128]])
                else:
                    dstap = hap(rp_sb, (g0 * 2 + dstpar) * RPW,
                                [[G * 2 * RPW, 128], [2 * RPW, 4], [1, w]])
                P.op(eng, lambda e, d=dstap, s=srcap: _copy(e, d, s), sem, 1)
            last_d, last_a = P.cnt["dve"], P.cnt["act"]
        return last_d, last_a

    ap_gate = [0, 0]

    def apply_pair(gc, lhsT, rhs_lo, rhs_hi, evac_fn):
        aset = gc % 2
        eng, sem = EV[aset]
        P.wait("tensor", sem, ap_gate[aset])
        for hh, rhs in ((0, rhs_lo), (1, rhs_hi)):
            P.op("tensor", lambda t, b=bank(2 * aset + hh), l=lhsT, r=rhs:
                 t.matmul(b, l, r, start=True, stop=True), "pe", 1)
        pe_here = P.cnt["pe"]
        P.wait(eng, "pe", pe_here)
        for hh in range(2):
            evac_fn(eng, sem, hh, bank(2 * aset + hh))
        ap_gate[aset] = P.cnt[sem]

    for grp in range(4):
        ms = grp % 2
        mat_fill(ms, I["su"], grp, 7)
        wide_fill(ms, I["wexpT"], grp)
        mat_st.wait("tensor", ms)
        ld_, la_ = chains(ms, 7, PLAN_UPL)
        P.wait("tensor", "dve", ld_)
        P.wait("tensor", "act", la_)
        wide_st.wait("tensor", ms)
        for gc in range(G):
            c = grp * G + gc
            lhsT = lct_sb[:, gc * 128:(gc + 1) * 128]

            def ev(eng, sem, hh, bank, c=c):
                P.op(eng, lambda e, d=u_sb[:, c * 1024 + hh * 512:
                                           c * 1024 + (hh + 1) * 512],
                     s=bank[:]: _copy(e, d, s), sem, 1)
            apply_pair(gc, lhsT,
                       wideA[ms][:, gc * 1024: gc * 1024 + 512],
                       wideA[ms][:, gc * 1024 + 512: gc * 1024 + 1024], ev)
        P.wait("tensor", "dve", P.cnt["dve"])
        P.wait("tensor", "act", P.cnt["act"])
        for gc in range(G):
            c = grp * G + gc
            P.op("tensor", lambda t, g2=gc, cc=c: t.matmul(
                ps_all[:, 4 * 512 + g2: 4 * 512 + g2 + 1],
                lct_sb[:, g2 * 128:(g2 + 1) * 128],
                bexp2d_sb[:, cc:cc + 1], start=True, stop=True), "pe", 1)
        P.wait("vector", "pe", P.cnt["pe"])
        P.op("vector", lambda v, c0=grp * G: v.tensor_copy(
            ub_sb[:, c0:c0 + G], ps_all[:, 4 * 512: 4 * 512 + G]), "dve", 1)
        mat_st.set_gate(ms, "pe", P.cnt["pe"])
        wide_st.set_gate(ms, "pe", P.cnt["pe"])

    u_dve, u_act = P.cnt["dve"], P.cnt["act"]
    ub_done = P.cnt["dve"]

    P.wait("gpsimd", "dve", ub_done)
    for pg in range(NPG):
        bsrc = hap(ub_sb, 4 * pg * NC, [[NC, 4], [1, NC]])
        P.op("gpsimd", lambda g, s=bsrc, d=bpt_sb[:, pg:pg + 1]:
             g.dma_start(out=d, in_=s), "grl", 16)
    n_grl_b = P.cnt["grl"]

    P.wait("gpsimd", "dve", u_dve)
    P.wait("gpsimd", "act", u_act)
    P.wait("sync", "dve", u_dve)
    P.wait("sync", "act", u_act)
    upt_next = [0]

    def ensure_upt(pg_to):
        while upt_next[0] <= min(pg_to, NPG - 1):
            pg = upt_next[0]
            slot = pg % 4
            src = hap(u_sb, 4 * pg * (NC * 1024), [[NC * 1024, 4], [1, NC * 1024]])
            upt_st.fill(slot, lambda g, s=src, d=upt_sl[slot][:]:
                        g.dma_start(out=d, in_=s))
            upt_next[0] += 1

    ensure_upt(3)
    for grp in range(4):
        ms = grp % 2
        mat_fill(ms, I["tu"], grp, 5)
        mat_st.wait("tensor", ms)
        ld_, la_ = chains(ms, 5, PLAN_UPH)
        P.wait("tensor", "dve", ld_)
        P.wait("tensor", "act", la_)
        if grp == 0:
            P.wait("tensor", "grl", n_grl_b)
        for gc in range(G):
            pg = grp * G + gc
            slot = pg % 4
            aset = gc % 2
            eng, sem = EV[aset]
            upt_st.wait("tensor", slot)
            P.wait("tensor", sem, ap_gate[aset])
            ssl = pg % 4
            w2w_st.wait(eng, ssl)
            ctw_st.wait(eng, ssl)
            for kd in range(KD):
                P.op("tensor", lambda t, b=bank(2 * aset + kd // 4),
                     k=kd, s2=slot, g2=gc: t.matmul(
                         b[:, (k % 4) * 128:(k % 4 + 1) * 128],
                         upt_sl[s2][:, k * 128:(k + 1) * 128],
                         lct_sb[:, g2 * 128:(g2 + 1) * 128],
                         start=True, stop=True), "pe", 1)
            pe_here = P.cnt["pe"]
            P.wait(eng, "pe", pe_here)
            for hh in range(2):
                P.op(eng, lambda e, s2=ssl, h2=hh, b=bank(2 * aset + hh):
                     _copy(e, stg[s2][:, h2 * 512:(h2 + 1) * 512], b), sem, 1)
            ap_gate[aset] = P.cnt[sem]
            upt_st.set_gate(slot, "pe", pe_here)
            P.op("tensor", lambda t, g2=gc, p2=pg: t.matmul(
                ps_all[:, 4 * 512 + g2: 4 * 512 + g2 + 1],
                lct_sb[:, g2 * 128:(g2 + 1) * 128],
                bpt_sb[:, p2:p2 + 1], start=True, stop=True), "pe", 1)
            P.wait("sync", sem, P.cnt[sem])
            w2w_st.fill(ssl, lambda sy, p2=pg, s2=ssl: sy.dma_start(
                out=w2t_s[p2], in_=stg[s2][:]))
            ensure_upt(pg + 4)
        P.wait("vector", "pe", P.cnt["pe"])
        P.op("vector", lambda v, c0=grp * G: v.scalar_tensor_tensor(
            bias2_sb[:, c0:c0 + G], ps_all[:, 4 * 512: 4 * 512 + G], ccol(CB_ONE),
            ubias_sb[:, c0:c0 + G], ALU.mult, ALU.add), "dve", 1)
        mat_st.set_gate(ms, "pe", P.cnt["pe"])

    ct_sb = u_sb
    for sl in range(4):
        upt_st.wait("gpsimd", sl)
    for grp in range(4):
        ms = grp % 2
        mat_fill(ms, I["tdt"], grp, 5)
        wide_fill(ms, I["wconpt"], grp)
        mat_st.wait("tensor", ms)
        ld_, la_ = chains(ms, 5, PLAN_DNH)
        P.wait("tensor", "dve", ld_)
        P.wait("tensor", "act", la_)
        wide_st.wait("tensor", ms)
        for gc in range(G):
            pg = grp * G + gc
            ssl = pg % 4
            eng, sem = EV[gc % 2]
            w2w_st.wait(eng, ssl)
            ctw_st.wait(eng, ssl)
            lhsT = lct_sb[:, gc * 128:(gc + 1) * 128]

            def ev(eng2, sem2, hh, bank, ssl=ssl):
                P.op(eng2, lambda e, d=stg[ssl][:, hh * 512:(hh + 1) * 512],
                     s=bank[:]: _copy(e, d, s), sem2, 1)
            apply_pair(gc, lhsT,
                       wideA[ms][:, gc * 1024: gc * 1024 + 512],
                       wideA[ms][:, gc * 1024 + 512: gc * 1024 + 1024], ev)
            sem = EV[gc % 2][1]
            P.wait("gpsimd", sem, P.cnt[sem])
            src = hap(stg[ssl], 0, [[1024, 128], [1, 1024]])
            dst = hap(ct_sb, 4 * pg * (NC * 1024), [[NC * 1024, 4], [1, NC * 1024]])
            ctw_st.fill(ssl, lambda g, s=src, d=dst: g.dma_start(out=d, in_=s))
        mat_st.set_gate(ms, "pe", P.cnt["pe"])
        wide_st.set_gate(ms, "pe", P.cnt["pe"])

    for sl in range(4):
        ctw_st.wait("tensor", sl)
    for grp in range(4):
        ms = grp % 2
        mat_fill(ms, I["sdt"], grp, 7)
        mat_st.wait("tensor", ms)
        ld_, la_ = chains(ms, 7, PLAN_DNL)
        P.wait("tensor", "dve", ld_)
        P.wait("tensor", "act", la_)
        for gc in range(G):
            c = grp * G + gc
            ssl = c % 4
            eng, sem = EV[gc % 2]
            ctw_st.wait(eng, ssl)
            w3w_st.wait(eng, ssl)
            lhsT = lct_sb[:, gc * 128:(gc + 1) * 128]

            def ev(eng2, sem2, hh, bank, ssl=ssl):
                P.op(eng2, lambda e, d=stg[ssl][:, hh * 512:(hh + 1) * 512],
                     s=bank[:]: _copy(e, d, s), sem2, 1)
            apply_pair(gc, lhsT,
                       ct_sb[:, c * 1024: c * 1024 + 512],
                       ct_sb[:, c * 1024 + 512: c * 1024 + 1024], ev)
            sem = EV[gc % 2][1]
            P.wait("sync", sem, P.cnt[sem])
            dstap = hap(w3a_s, c * 1024,
                        [[128 * 1024, 32], [32 * 1024, 4], [1, 1024]])
            srcap = hap(stg[ssl], 0, [[1024, 128], [1, 1024]])
            w3w_st.fill(ssl, lambda sy, d=dstap, s=srcap: sy.dma_start(
                out=d, in_=s))
        mat_st.set_gate(ms, "pe", P.cnt["pe"])

    if debug:
        for st4 in (w2w_st, w3w_st):
            for sl in range(4):
                st4.wait("gpsimd", sl)
        P.op("gpsimd", lambda g: g.dma_start(out=DBG["dbg_w2t"][:], in_=w2t_s[:]),
             "sdbg", 16)
        P.op("gpsimd", lambda g: g.dma_start(out=DBG["dbg_w3a"][:], in_=w3a_s[:]),
             "sdbg", 16)
        P.wait("gpsimd", "dve", P.cnt["dve"])
        P.op("gpsimd", lambda g: g.dma_start(out=DBG["dbg_bias2"][:], in_=bias2_sb[:]),
             "sdbg", 16)
    P.flush("p1")
    fr.close()

    w2_store_cnt = [P.cnt[f"sw2{i}"] for i in range(4)]
    w3_store_cnt = [P.cnt[f"sw3{i}"] for i in range(4)]
    prep_dve = P.cnt["dve"]

    fr = ExitStack()
    xload = [fr.enter_context(nc.sbuf_tensor(f"xl{i}", [128, D], F32))
             for i in range(2)]
    xsq = fr.enter_context(nc.sbuf_tensor("xsq", [128, D], F32))
    stats = fr.enter_context(nc.sbuf_tensor("stats", [128, 16], F32))
    xnbf = fr.enter_context(nc.sbuf_tensor("xnbf", [128, D], BF16))
    xn1t = fr.enter_context(nc.sbuf_tensor("xn1t", [128, KD * T], BF16))
    wv_bf = fr.enter_context(nc.sbuf_tensor("wv_bf", [128, KD * D], BF16))
    qt_sb = fr.enter_context(nc.sbuf_tensor("qt_sb", [128, 8 * T], BF16))
    kt_sb = fr.enter_context(nc.sbuf_tensor("kt_sb", [128, 8 * T], BF16))
    VROW = NH * 65
    v_sb = fr.enter_context(nc.sbuf_tensor("v_sb", [128, TT * VROW], BF16))
    wst = [fr.enter_context(nc.sbuf_tensor(f"wst{i}", [128, KD * 128], BF16))
           for i in range(2)]
    pt_sb = fr.enter_context(nc.sbuf_tensor("pt_sb", [128, 6 * 512], BF16))
    nsb = fr.enter_context(nc.sbuf_tensor("nsb", [128, 2 * 512], F32))
    rec_sb = fr.enter_context(nc.sbuf_tensor("rec_sb", [128, 8], F32))
    stage = fr.enter_context(nc.sbuf_tensor("stage", [128, 4 * D], F32))
    r1w = [fr.enter_context(nc.sbuf_tensor(f"r1w{i}", [128, D], F32))
           for i in range(2)]

    x_st = Stream(P, "lx", "sync")
    r1_st = Stream(P, "sr1", "sync")

    pfr = ExitStack()
    tp_ps = [pfr.enter_context(nc.psum_tensor(f"tpps{i}", [128, 1024], BF16))
             for i in range(2)]
    P.begin()
    P.frame_barrier(["sw20", "sw21", "sw22", "sw23", "sw30", "sw31", "sw32",
                     "sw33", "gu0", "gu1", "gu2", "gu3", "gc0", "gc1", "gc2",
                     "gc3", "gm0", "gm1", "gw0", "gw1", "grl", "srl", "sdbg"])
    for kd in range(KD):
        P.op("gpsimd", lambda g, k=kd: g.dma_start(
            out=wv_bf[:, k * D:(k + 1) * D], in_=I["wv"][k]), "g1", 16)
    n_g1_wv = P.cnt["g1"]
    for t_i in range(TT):
        slot = t_i % 2
        x_st.fill(slot, lambda sy, ti=t_i, s=slot: sy.dma_start(
            out=xload[s][:], in_=I["x"][ti * 128:(ti + 1) * 128, :]))
        x_st.wait("scalar", slot)
        x_st.wait("vector", slot)
        ln_tile(xload[slot][:], xn1t, T, t_i * 128, xsq, stats, xnbf,
                tp_ps[t_i % 2])
        x_st.set_gate(slot, "dve", P.cnt["dve"])
    P.flush("p2")
    pfr.close()

    pfr = ExitStack()
    ps_qk = [pfr.enter_context(nc.psum_tensor(f"psqk{i}", [128, 512], F32))
             for i in range(4)]
    P.begin()
    P.op("vector", lambda v: v.memset(
        hap(v_sb, 64, [[TT * VROW, 128], [VROW, TT], [65, NH]]), 1.0), "dve", 1)
    P.wait("tensor", "g1", n_g1_wv)
    P.wait("tensor", "dve", P.cnt["dve"])
    bank_gate = [0, 0, 0, 0]
    for t_i in range(TT):
        for hh in range(2):
            bi = (t_i % 2) * 2 + hh
            bank = ps_qk[bi]
            P.wait("tensor", "dve", bank_gate[bi])
            for kd in range(KD):
                P.op("tensor", lambda t, ti=t_i, k=kd, h=hh, b=bank: t.matmul(
                    b[:], xn1t[:, k * T + ti * 128: k * T + (ti + 1) * 128],
                    wv_bf[:, k * D + h * 512: k * D + (h + 1) * 512],
                    start=(k == 0), stop=(k == KD - 1)), "pe", 1)
            P.wait("vector", "pe", P.cnt["pe"])
            P.op("vector", lambda v, ti=t_i, h=hh, b=bank: v.tensor_copy(
                hap(v_sb, ti * VROW + h * 8 * 65, [[TT * VROW, 128], [65, 8], [1, 64]]),
                hap(b, 0, [[512, 128], [64, 8], [1, 64]])), "dve", 1)
            bank_gate[bi] = P.cnt["dve"]
    wq_st = Stream(P, "gq", "gpsimd")
    for qf in range(16):
        slot = qf % 2
        wq_st.fill(slot, lambda g, q=qf, s=slot: g.dma_start(
            out=wst[s][:], in_=hap(I["wqk"], q * KD * 16384,
                                   [[128, 128], [16384, KD], [1, 128]])))
        wq_st.wait("tensor", slot)
        dst_sb = qt_sb if qf < 8 else kt_sb
        hp = qf % 8
        for ts in range(4):
            bi = slot * 2 + (ts % 2)
            bank = ps_qk[bi]
            P.wait("tensor", "dve", bank_gate[bi])
            for kd in range(KD):
                P.op("tensor", lambda t, s=slot, k=kd, tss=ts, b=bank: t.matmul(
                    b[:], wst[s][:, k * 128:(k + 1) * 128],
                    xn1t[:, k * T + tss * 512: k * T + (tss + 1) * 512],
                    start=(k == 0), stop=(k == KD - 1)), "pe", 1)
            P.wait("vector", "pe", P.cnt["pe"])
            P.op("vector", lambda v, d=dst_sb, h=hp, tss=ts, b=bank: v.tensor_copy(
                d[:, h * T + tss * 512: h * T + (tss + 1) * 512], b[:]), "dve", 1)
            bank_gate[bi] = P.cnt["dve"]
        wq_st.set_gate(slot, "pe", P.cnt["pe"])
    if debug:
        P.wait("gpsimd", "dve", P.cnt["dve"])
        P.op("gpsimd", lambda g: g.dma_start(out=DBG["dbg_xn1t"][:], in_=xn1t[:]),
             "sdbg", 16)
        P.op("gpsimd", lambda g: g.dma_start(out=DBG["dbg_qt"][:], in_=qt_sb[:]),
             "sdbg", 16)
        P.op("gpsimd", lambda g: g.dma_start(out=DBG["dbg_kt"][:], in_=kt_sb[:]),
             "sdbg", 16)
        P.op("gpsimd", lambda g: g.dma_start(out=DBG["dbg_v"][:], in_=v_sb[:]),
             "sdbg", 16)
    P.flush("p3")
    pfr.close()

    pfr = ExitStack()
    ps_s = pfr.enter_context(nc.psum_tensor("ps_s", [128, 2048], F32))
    ps_av = pfr.enter_context(nc.psum_tensor("ps_av", [128, 1024], F32))
    ps_avt = [pfr.enter_context(nc.psum_tensor(f"psavt{i}", [128, 512], F32))
              for i in range(2)]
    P.begin()
    P.wait("tensor", "dve", P.cnt["dve"])
    P.wait("scalar", "dve", P.cnt["dve"])

    exp_done = [0, 0, 0]
    pt_gate = [0, 0, 0]
    sbank_gate = [0, 0]
    mul_dve = 0
    pend_epi = None

    def emit_av(pend, hp, av_started):
        p3, jt, is_last = pend
        P.wait("tensor", "act", exp_done[p3])
        for eo in range(2):
            h = 2 * hp + eo
            P.op("tensor", lambda t, e=eo, pp=p3, j=jt, hh=h, last=is_last: t.matmul(
                ps_av[0:65, e * 512:(e + 1) * 512],
                v_sb[:, j * VROW + hh * 65: j * VROW + hh * 65 + 65],
                pt_sb[:, (pp * 2 + e) * 512:(pp * 2 + e + 1) * 512],
                start=(not av_started), stop=last), "pe", 1)

    mul_dve_box = [0]

    def emit_epilogue(epi):
        hp, pe_av = epi
        P.wait("scalar", "pe", pe_av)
        P.op("scalar", lambda s: s.copy(nsb[0:65, :], ps_av[0:65, :]), "act", 1)
        P.wait("tensor", "act", P.cnt["act"])
        P.wait("tensor", "dve", mul_dve_box[0])
        for eo in range(2):
            for t4 in range(4):
                P.op("tensor", lambda t, e=eo, q=t4: t.transpose(
                    ps_avt[e][:, q * 128:q * 128 + 65],
                    nsb[0:65, e * 512 + q * 128: e * 512 + (q + 1) * 128],
                    ident_f[0:65, 0:65]), "pe", 1)
        P.wait("vector", "pe", P.cnt["pe"])
        for eo in range(2):
            P.op("vector", lambda v, e=eo: v.reciprocal(
                hap(rec_sb, e * 4, [[8, 128], [1, 4]]),
                hap(ps_avt[e], 64, [[512, 128], [128, 4], [1, 1]])))
        P.drain("vector")
        for eo in range(2):
            h = 2 * hp + eo
            for t4 in range(4):
                P.op("vector", lambda v, e=eo, q=t4, hh=h: v.tensor_scalar(
                    stage[:, q * D + hh * 64: q * D + hh * 64 + 64],
                    ps_avt[e][:, q * 128: q * 128 + 64],
                    rec_sb[:, e * 4 + q: e * 4 + q + 1], None, ALU.mult),
                    "dve", 1)
        mul_dve_box[0] = P.cnt["dve"]

    for ts in range(4):
        jmax = 4 * ts + 4
        for hp in range(8):
            av_started = False
            pends = []
            for jt in range(jmax):
                r = jt % 2
                p3 = jt % 3
                P.wait("tensor", "act", sbank_gate[r])
                diag = jt >= 4 * ts
                for eo in range(2):
                    P.op("tensor", lambda t, e=eo, j=jt, h=hp, tss=ts, rr=r, dg=diag:
                         t.matmul(
                             ps_s[:, (rr * 2 + e) * 512:(rr * 2 + e + 1) * 512],
                             kt_sb[e * 64:(e + 1) * 64,
                                   h * T + j * 128: h * T + (j + 1) * 128],
                             qt_sb[e * 64:(e + 1) * 64,
                                   h * T + tss * 512: h * T + (tss + 1) * 512],
                             start=True, stop=not dg,
                             tile_position=(e * 64, 0)), "pe", 1)
                if diag:
                    off4 = jt - 4 * ts
                    for eo in range(2):
                        P.op("tensor", lambda t, e=eo, rr=r, o4=off4: t.matmul(
                            ps_s[:, (rr * 2 + e) * 512:(rr * 2 + e + 1) * 512],
                            ident_bf[:], maskw_sb[:, o4 * 512:(o4 + 1) * 512],
                            start=False, stop=True), "pe", 1)
                pe_scores = P.cnt["pe"]
                if jt == 0 and pend_epi is not None:
                    emit_epilogue(pend_epi)
                    pend_epi = None
                if len(pends) >= 2:
                    emit_av(pends.pop(0), hp, av_started)
                    av_started = True
                    pt_gate[(jt - 2) % 3] = P.cnt["pe"]
                P.wait("scalar", "pe", pe_scores)
                P.wait("scalar", "pe", pt_gate[p3])
                P.op("scalar", lambda s, rr=r, pp=p3: s.activation(
                    pt_sb[:, pp * 1024:(pp + 1) * 1024],
                    ps_s[:, rr * 1024:(rr + 1) * 1024], AF.Exp,
                    bias=ccol(CB_EXPB), scale=ccol(CB_EXPS)), "act", 1)
                exp_done[p3] = P.cnt["act"]
                sbank_gate[r] = P.cnt["act"]
                pends.append((p3, jt, jt == jmax - 1))
            for pend in pends:
                emit_av(pend, hp, av_started)
                av_started = True
                pt_gate[pend[0]] = P.cnt["pe"]
            pends = []
            pend_epi = (hp, P.cnt["pe"])
        emit_epilogue(pend_epi)
        pend_epi = None
        for t4 in range(4):
            t_i = ts * 4 + t4
            slot = t_i % 2
            x_st.fill(slot, lambda sy, ti=t_i, s=slot: sy.dma_start(
                out=xload[s][:], in_=I["x"][ti * 128:(ti + 1) * 128, :]))
            x_st.wait("vector", slot)
            r1_st.wait("vector", slot)
            P.drain("vector")
            P.op("vector", lambda v, q=t4, s=slot: v.tensor_tensor(
                r1w[s][:], stage[:, q * D:(q + 1) * D], xload[s][:], ALU.add),
                "dve", 1)
            x_st.set_gate(slot, "dve", P.cnt["dve"])
            P.wait("sync", "dve", P.cnt["dve"])
            r1_st.fill(slot, lambda sy, ti=t_i, s=slot: sy.dma_start(
                out=r1_s[ti * 128:(ti + 1) * 128, :], in_=r1w[s][:]))
        P.drain("vector")
    if debug:
        r1_st.wait("gpsimd", 0)
        r1_st.wait("gpsimd", 1)
        P.op("gpsimd", lambda g: g.dma_start(out=DBG["dbg_r1"][:], in_=r1_s[:]),
             "sdbg", 16)
    P.flush("p4")
    pfr.close()
    fr.close()

    fr = ExitStack()
    r1h = fr.enter_context(nc.sbuf_tensor("r1h", [128, 8 * D], F32))
    xsq2 = fr.enter_context(nc.sbuf_tensor("xsq2", [128, D], F32))
    stats2 = fr.enter_context(nc.sbuf_tensor("stats2", [128, 16], F32))
    xnbf2 = fr.enter_context(nc.sbuf_tensor("xnbf2", [128, D], BF16))
    xn2t = fr.enter_context(nc.sbuf_tensor("xn2t", [128, KD * 1024], BF16))
    h4 = fr.enter_context(nc.sbuf_tensor("h4", [128, NC * 1024], BF16))
    w2sb = [fr.enter_context(nc.sbuf_tensor(f"w2sb{i}", [128, 1024], BF16))
            for i in range(2)]
    w3sb = [fr.enter_context(nc.sbuf_tensor(f"w3sb{i}", [128, 1024], BF16))
            for i in range(2)]
    outst = [fr.enter_context(nc.sbuf_tensor(f"outst{i}", [128, D], F32))
             for i in range(2)]
    w3res = fr.enter_context(nc.sbuf_tensor("w3res", [128, 28 * 1024], BF16))

    w2r_st = Stream(P, "lw2", "sync")
    w3r_st = Stream(P, "lw3", "sync")
    y_st = Stream(P, "sy", "sync")
    r1l_gate = [0, 0]

    for half in range(2):
        pfr = ExitStack()
        tp2 = [pfr.enter_context(nc.psum_tensor(f"tp2_{half}_{i}", [128, 1024], BF16))
               for i in range(2)]
        P.begin()
        if half == 0:
            P.frame_barrier(["sr10", "sr11", "lx0", "lx1", "gq0", "gq1",
                             "g1", "sdbg"])
            if debug:
                for e in ENGINES:
                    P.wait(e, "sdbg", P.cnt["sdbg"])
            for i4 in range(4):
                P.wait("sync", f"sw2{i4}", w2_store_cnt[i4])
                P.wait("sync", f"sw3{i4}", w3_store_cnt[i4])
            P.wait("sync", "sr10", P.cnt["sr10"])
            P.wait("sync", "sr11", P.cnt["sr11"])
            P.sem("sw3r")
            for i4 in range(4):
                P.wait("scalar", f"sw3{i4}", w3_store_cnt[i4])
            for p in range(28):
                P.op("scalar", lambda s, pp=p: s.dma_start(
                    out=w3res[:, pp * 1024:(pp + 1) * 1024],
                    in_=w3a_s[pp]), "sw3r", 16)
        else:
            P.wait("vector", "pe", P.cnt["pe"])
        for tl in range(8):
            t_i = half * 8 + tl
            slot = tl % 2
            P.gwait("sync", ("dve", r1l_gate[slot]) if r1l_gate[slot] else None)
            x_st.fill(slot, lambda sy, ti=t_i, l=tl: sy.dma_start(
                out=r1h[:, l * D:(l + 1) * D], in_=r1_s[ti * 128:(ti + 1) * 128, :]))
            x_st.wait("scalar", slot)
            x_st.wait("vector", slot)
            r1tile = r1h[:, tl * D:(tl + 1) * D]
            ln_tile(r1tile, xn2t, 1024, tl * 128, xsq2, stats2, xnbf2, tp2[tl % 2])
            x_st.set_gate(slot, "dve", P.cnt["dve"])
            P.drain("vector")
            P.op("vector", lambda v, rt=r1tile: v.tensor_tensor(
                rt, rt, bias3rep[:], ALU.add), "dve", 1)
        P.flush(f"p5a{half}")
        pfr.close()

        pfr = ExitStack()
        ps_up = [pfr.enter_context(nc.psum_tensor(f"psup_{half}_{i}", [128, 512], F32))
                 for i in range(2)]
        P.begin()
        if debug and half == 1:
            P.wait("scalar", "sdbg", P.cnt["sdbg"])
        P.wait("tensor", "dve", P.cnt["dve"])
        P.wait("scalar", "dve", prep_dve)
        up_gate = [0, 0]
        for pg in range(NPG):
            slot = pg % 2
            w2r_st.fill(slot, lambda sy, p=pg, s=slot: sy.dma_start(
                out=w2sb[s][:], in_=w2t_s[p]))
            w2r_st.wait("tensor", slot)
            for ts2 in range(2):
                bank = ps_up[ts2]
                P.wait("tensor", "act", up_gate[ts2])
                for kd in range(KD):
                    P.op("tensor", lambda t, s=slot, k=kd, t2=ts2, b=bank: t.matmul(
                        b[:], w2sb[s][:, k * 128:(k + 1) * 128],
                        xn2t[:, k * 1024 + t2 * 512: k * 1024 + (t2 + 1) * 512],
                        start=(k == 0), stop=(k == KD - 1)), "pe", 1)
                P.wait("scalar", "pe", P.cnt["pe"])
                P.op("scalar", lambda s, p=pg, t2=ts2, b=bank: s.activation(
                    h4[:, p * 1024 + t2 * 512: p * 1024 + (t2 + 1) * 512],
                    b[:], AF.Gelu, bias=bias2_sb[:, p:p + 1]), "act", 1)
                up_gate[ts2] = P.cnt["act"]
            w2r_st.set_gate(slot, "pe", P.cnt["pe"])
        P.flush(f"p5b{half}")
        pfr.close()

        pfr = ExitStack()
        ps_dn = [pfr.enter_context(nc.psum_tensor(f"psdn_{half}_{i}", [128, 512], F32))
                 for i in range(8)]
        P.begin()
        P.wait("tensor", "act", P.cnt["act"])
        P.wait("tensor", "sw3r", P.cnt["sw3r"])
        dn_gates = [0] * 8
        for ttg in range(2):
            for pg in range(NPG):
                if pg < 28:
                    wsrc, woff = w3res, pg * 1024
                else:
                    slot = pg % 2
                    w3r_st.fill(slot, lambda sy, p=pg, s=slot: sy.dma_start(
                        out=w3sb[s][:], in_=w3a_s[p]))
                    w3r_st.wait("tensor", slot)
                    wsrc, woff = w3sb[slot], 0
                for tt in range(4):
                    tloc = ttg * 4 + tt
                    for ds in range(2):
                        bank = ps_dn[tt * 2 + ds]
                        if pg == 0:
                            P.wait("tensor", "dve", dn_gates[tt * 2 + ds])
                        P.op("tensor",
                             lambda t, p=pg, w=wsrc, wo=woff, tl=tloc, d=ds,
                             b=bank: t.matmul(
                                 b[:],
                                 h4[:, p * 1024 + tl * 128: p * 1024 + (tl + 1) * 128],
                                 w[:, wo + d * 512: wo + (d + 1) * 512],
                                 start=(p == 0), stop=(p == NPG - 1)), "pe", 1)
                if pg >= 28:
                    w3r_st.set_gate(slot, "pe", P.cnt["pe"])
            pe_dn = P.cnt["pe"]
            P.wait("vector", "pe", pe_dn)
            for tt in range(4):
                tloc = ttg * 4 + tt
                t_i = half * 8 + tloc
                oslot = tloc % 2
                y_st.wait("vector", oslot)
                for ds in range(2):
                    P.op("vector",
                         lambda v, tl=tloc, d=ds, o=oslot, b=ps_dn[tt * 2 + ds]:
                         v.scalar_tensor_tensor(
                             outst[o][:, d * 512:(d + 1) * 512], b[:], ccol(CB_ONE),
                             r1h[:, tl * D + d * 512: tl * D + d * 512 + 512],
                             ALU.mult, ALU.add), "dve", 1)
                    dn_gates[tt * 2 + ds] = P.cnt["dve"]
                P.wait("sync", "dve", P.cnt["dve"])
                y_st.fill(oslot, lambda sy, ti=t_i, o=oslot: sy.dma_start(
                    out=y[ti * 128:(ti + 1) * 128, :], in_=outst[o][:]))
        r1l_gate = [P.cnt["dve"], P.cnt["dve"]]
        if debug and half == 0:
            P.wait("gpsimd", "act", P.cnt["act"])
            P.op("gpsimd", lambda g: g.dma_start(out=DBG["dbg_h4"][:], in_=h4[:]),
                 "sdbg", 16)
        P.flush(f"p5c{half}")
        pfr.close()

    P.begin()
    P.wait("sync", "sy0", P.cnt["sy0"])
    P.wait("sync", "sy1", P.cnt["sy1"])
    if debug:
        P.wait("gpsimd", "sdbg", P.cnt["sdbg"])
    P.flush("pfin")
    fr.close()
    st.close()
    return nc


def kernel(L0_attn_x, L0_attn_W_qkv, L0_ffn_W_exp, L0_ffn_b_exp,
           L0_ffn_up_weights, L0_ffn_up_bias, L0_ffn_down_weights,
           L0_ffn_W_con, L0_ffn_b_con, L0_ffn_down_bias):
    x = np.ascontiguousarray(np.asarray(L0_attn_x, dtype=np.float32))
    host = _host_arrays(np.asarray(L0_attn_W_qkv, np.float32),
                        np.asarray(L0_ffn_W_exp, np.float32),
                        np.asarray(L0_ffn_b_exp, np.float32),
                        np.asarray(L0_ffn_up_weights, np.float32),
                        np.asarray(L0_ffn_up_bias, np.float32),
                        np.asarray(L0_ffn_down_weights, np.float32),
                        np.asarray(L0_ffn_W_con, np.float32),
                        np.asarray(L0_ffn_b_con, np.float32),
                        np.asarray(L0_ffn_down_bias, np.float32))
    nc = build_nc()
    in_maps = [dict(host, x=np.ascontiguousarray(x[b])) for b in range(8)]
    res = run_bass_kernel_spmd(nc, in_maps, core_ids=list(range(8)))
    return np.stack([res.results[b]["y"] for b in range(8)], axis=0)
